# version 01 (final)
# baseline (speedup 1.0000x reference)
import contextlib
import numpy as np
import concourse.bass as bass
import concourse.mybir as mybir
from concourse.bass_utils import run_bass_kernel_spmd
from concourse.bass import IndirectOffsetOnAxis

F32 = mybir.dt.float32
BF16 = mybir.dt.bfloat16
I32 = mybir.dt.int32
AF = mybir.ActivationFunctionType
ALU = mybir.AluOpType
AX = mybir.AxisListType

ENGINES = ("tensor", "vector", "scalar", "gpsimd", "sync")
NCORE = 8


class Prog:

    def __init__(self):
        self.ops = {e: [] for e in ENGINES}
        self.bar_total = 0
        self.stage_dma = 0
        self.active = {e: False for e in ENGINES}
        self.synced = {e: 0 for e in ENGINES}
        self.streams = {}
        self.stream_waited = {e: {} for e in ENGINES}

    def _lazy_wait(self, eng):
        if self.synced[eng] < self.bar_total:
            tot = self.bar_total
            self.ops[eng].append({"fn": (lambda e, tot=tot: e.wait_ge(e._bar, tot)),
                                  "inc": None, "wait": True})
            self.synced[eng] = tot

    def on(self, eng, fn):
        self._lazy_wait(eng)
        self.ops[eng].append({"fn": fn, "inc": None, "wait": False})
        self.active[eng] = True
        return self

    def dma(self, eng, fn):
        self._lazy_wait(eng)
        self.ops[eng].append({"fn": fn, "inc": ("bar", 16), "wait": False})
        self.stage_dma += 1
        return self

    def sdma(self, eng, stream, fn):
        self.ops[eng].append({"fn": fn, "inc": (stream, 16), "wait": False})
        self.streams[stream] = self.streams.get(stream, 0) + 16
        return self.streams[stream]

    def qwait(self, eng, tot=None):
        tot = self.bar_total if tot is None else tot
        if self.synced[eng] >= tot:
            return self
        self.ops[eng].append({"fn": (lambda e, tot=tot: e.wait_ge(e._bar, tot)),
                              "inc": None, "wait": True})
        self.synced[eng] = max(self.synced[eng], tot)
        return self

    def wait_stream(self, eng, stream, count=None):
        if count is None:
            count = self.streams.get(stream, 0)
        w = self.stream_waited[eng]
        if w.get(stream, -1) >= count:
            return self
        w[stream] = count
        self.ops[eng].append({"fn": (lambda e, s=stream, c=count: e.wait_ge(e._sems[s], c)),
                              "inc": None, "wait": True})
        return self

    def barrier(self):
        add = self.stage_dma * 16
        for e in ENGINES:
            if self.active[e]:
                for d in reversed(self.ops[e]):
                    if d["inc"] is None and not d["wait"]:
                        d["inc"] = ("bar", 1)
                        add += 1
                        break
                self.active[e] = False
        self.bar_total += add
        self.stage_dma = 0
        return self

    def build(self, nc):
        self.barrier()
        stream_names = sorted(self.streams)
        with nc.Block() as block:
            with contextlib.ExitStack() as stack:
                bar = stack.enter_context(nc.semaphore("bar"))
                sems = {s: stack.enter_context(nc.semaphore("st_" + s))
                        for s in stream_names}
                for ename in ENGINES:
                    if not self.ops[ename]:
                        continue
                    def run_ops(eng, ops=self.ops[ename]):
                        eng._bar = bar
                        eng._sems = sems
                        for d in ops:
                            instr = d["fn"](eng)
                            if d["inc"] is not None:
                                sem = bar if d["inc"][0] == "bar" else sems[d["inc"][0]]
                                instr.then_inc(sem, d["inc"][1])
                    getattr(block, ename)(run_ops)


def _mk_ident(P, ident):
    P.on("gpsimd", lambda e: e.memset(ident[:], 0.0))
    P.barrier()
    P.on("gpsimd", lambda e: e.affine_select(out=ident[:], in_=ident[:],
            compare_op=ALU.not_equal, fill=1.0, base=0,
            pattern=[[-1, ident.shape[1]]], channel_multiplier=1))
    P.barrier()


def emit_ln_fm(P, nc, x_dram, xst, sqt, statA, statB, rstdB, mrsB, xln, ones_col, ones_row,
               pbank, epsc, dinv=1.0 / 2048.0, T=1024, KB=16, stream="wx",
               qeng="sync", out_dt=BF16, apply=True):
    NCH = KB * (T // 512)
    nh = T // 512
    P.qwait(qeng)
    cnts = []
    for ch in range(min(2, NCH)):
        kb, h = divmod(ch, nh)
        cnts.append(P.sdma(qeng, stream,
            lambda e, kb=kb, h=h, ch=ch: e.dma_start(
                out=xst[:, (ch % 2) * 512:(ch % 2) * 512 + 512],
                in_=x_dram[kb * 128:(kb + 1) * 128, h * 512:(h + 1) * 512])))
    for ch in range(NCH):
        kb, h = divmod(ch, nh)
        slot = ch % 2
        P.wait_stream("scalar", stream, cnts[ch])
        P.on("scalar", lambda e, slot=slot: e.activation(
            out=sqt[:, slot * 512:slot * 512 + 512],
            in_=xst[:, slot * 512:slot * 512 + 512], func=AF.Square))
        P.barrier()
        st = (kb == 0)
        sp = (kb == KB - 1)
        P.on("tensor", lambda e, h=h, slot=slot, st=st, sp=sp: e.matmul(
            pbank[h][0:1, :], ones_col[:], xst[:, slot * 512:slot * 512 + 512],
            start=st, stop=sp))
        P.on("tensor", lambda e, h=h, slot=slot, st=st, sp=sp, nh=nh: e.matmul(
            pbank[nh + h][0:1, :], ones_col[:], sqt[:, slot * 512:slot * 512 + 512],
            start=st, stop=sp))
        P.barrier()
        if ch + 2 < NCH:
            kb2, h2 = divmod(ch + 2, nh)
            P.qwait(qeng)
            cnts.append(P.sdma(qeng, stream,
                lambda e, kb2=kb2, h2=h2, slot=slot: e.dma_start(
                    out=xst[:, slot * 512:slot * 512 + 512],
                    in_=x_dram[kb2 * 128:(kb2 + 1) * 128, h2 * 512:(h2 + 1) * 512])))
    mean = rstdB[0:1, 0:T]
    ms = mrsB[0:1, 0:T]
    rstd = statA[0:1, 0:T]
    mrs = statB[0:1, 0:T]
    for h in range(nh):
        P.on("vector", lambda e, h=h: e.tensor_copy(mean[:, h * 512:h * 512 + 512],
                                                    pbank[h][0:1, :]))
        P.on("scalar", lambda e, h=h, nh=nh: e.activation(out=ms[:, h * 512:h * 512 + 512],
                in_=pbank[nh + h][0:1, :], func=AF.Copy, scale=dinv))
    P.barrier()
    P.on("scalar", lambda e: e.activation(out=mean[:], in_=mean[:], func=AF.Copy,
                                          scale=dinv))
    P.barrier()
    P.on("scalar", lambda e: e.activation(out=rstd[:], in_=mean[:], func=AF.Square))
    P.barrier()
    P.on("vector", lambda e: e.tensor_tensor(out=rstd[:], in0=ms[:], in1=rstd[:],
                                             op=ALU.subtract))
    P.barrier()
    P.on("scalar", lambda e: e.activation(out=ms[:], in_=rstd[:], func=AF.Sqrt,
                                          bias=epsc[:]))
    P.barrier()
    P.on("vector", lambda e: e.reciprocal(rstd[:], ms[:]))
    P.barrier()
    P.on("vector", lambda e: e.tensor_tensor(out=mrs[:], in0=mean[:], in1=rstd[:],
                                             op=ALU.mult))
    P.barrier()
    for h in range(nh):
        P.on("tensor", lambda e, h=h: e.matmul(pbank[h][:],
                ones_row[:], rstd[:, h * 512:h * 512 + 512], start=True, stop=True))
        P.on("tensor", lambda e, h=h, nh=nh: e.matmul(pbank[nh + h][:],
                ones_row[:], mrs[:, h * 512:h * 512 + 512], start=True, stop=True))
    P.barrier()
    for h in range(nh):
        P.on("vector", lambda e, h=h: e.tensor_copy(rstdB[:, h * 512:h * 512 + 512],
                                                    pbank[h][:]))
        P.on("vector", lambda e, h=h, nh=nh: e.tensor_copy(mrsB[:, h * 512:h * 512 + 512],
                                                    pbank[nh + h][:]))
    P.barrier()
    if not apply:
        return
    P.qwait(qeng)
    cnts2 = []
    for ch in range(min(2, NCH)):
        kb, h = divmod(ch, nh)
        cnts2.append(P.sdma(qeng, stream,
            lambda e, kb=kb, h=h, ch=ch: e.dma_start(
                out=xst[:, (ch % 2) * 512:(ch % 2) * 512 + 512],
                in_=x_dram[kb * 128:(kb + 1) * 128, h * 512:(h + 1) * 512])))
    for ch in range(NCH):
        kb, h = divmod(ch, nh)
        slot = ch % 2
        P.wait_stream("vector", stream, cnts2[ch])
        P.on("vector", lambda e, slot=slot, h=h: e.tensor_tensor(
            out=sqt[:, slot * 512:slot * 512 + 512],
            in0=xst[:, slot * 512:slot * 512 + 512],
            in1=rstdB[:, h * 512:h * 512 + 512], op=ALU.mult))
        P.barrier()
        P.on("vector", lambda e, slot=slot, kb=kb, h=h: e.tensor_tensor(
            out=xln[:, kb * T + h * 512:kb * T + h * 512 + 512],
            in0=sqt[:, slot * 512:slot * 512 + 512],
            in1=mrsB[:, h * 512:h * 512 + 512], op=ALU.subtract))
        P.barrier()
        if ch + 2 < NCH:
            kb2, h2 = divmod(ch + 2, nh)
            P.qwait(qeng)
            cnts2.append(P.sdma(qeng, stream,
                lambda e, kb2=kb2, h2=h2, slot=slot: e.dma_start(
                    out=xst[:, slot * 512:slot * 512 + 512],
                    in_=x_dram[kb2 * 128:(kb2 + 1) * 128, h2 * 512:(h2 + 1) * 512])))


def build_l0ffn(dump=False):
    nc = bass.Bass("TRN2", target_bir_lowering=False)
    T, D, KB, NE, MBI, MBD, MBS = 1024, 2048, 16, 8, 11, 16, 22
    xt_d = nc.dram_tensor("xt", [D, T], F32, kind="ExternalInput").ap()
    gw_d = nc.dram_tensor("gw", [NE, MBI, 128, KB, 128], F32, kind="ExternalInput").ap()
    uw_d = nc.dram_tensor("uw", [NE, MBI, 128, KB, 128], F32, kind="ExternalInput").ap()
    dw_d = nc.dram_tensor("dw", [NE, MBD, 128, MBI, 128], F32, kind="ExternalInput").ap()
    sgw_d = nc.dram_tensor("sgw", [MBS, 128, KB, 128], F32, kind="ExternalInput").ap()
    suw_d = nc.dram_tensor("suw", [MBS, 128, KB, 128], F32, kind="ExternalInput").ap()
    sdw_d = nc.dram_tensor("sdw", [MBD, 128, MBS, 128], F32, kind="ExternalInput").ap()
    sgb_d = nc.dram_tensor("sgb", [128, MBS], F32, kind="ExternalInput").ap()
    sub_d = nc.dram_tensor("sub", [128, MBS], F32, kind="ExternalInput").ap()
    sdb_d = nc.dram_tensor("sdb", [128, MBD], F32, kind="ExternalInput").ap()
    rwt_d = nc.dram_tensor("rwt", [D, NE], F32, kind="ExternalInput").ap()
    out_d = nc.dram_tensor("out", [D, T], F32, kind="ExternalOutput").ap()
    if dump:
        dxln_d = nc.dram_tensor("dxln", [128, KB * T], BF16, kind="ExternalOutput").ap()
        dlg_d = nc.dram_tensor("dlg", [8, T], F32, kind="ExternalOutput").ap()
        dcomb_d = nc.dram_tensor("dcomb", [8, T], BF16, kind="ExternalOutput").ap()
        dacc_d = nc.dram_tensor("dacc", [128, MBD * T], F32, kind="ExternalOutput").ap()
        dG_d = nc.dram_tensor("dG", [128, MBI * T], BF16, kind="ExternalOutput").ap()
        dacc2_d = nc.dram_tensor("dacc2", [128, MBD * T], F32, kind="ExternalOutput").ap()

    P = Prog()
    with contextlib.ExitStack() as stack:
        sb = lambda n, s, dt=F32: stack.enter_context(nc.sbuf_tensor(n, s, dt))
        ps = lambda n, s: stack.enter_context(nc.psum_tensor(n, s, F32))
        xst = sb("xst", [128, 1024])
        sqt = sb("sqt", [128, 1024])
        statA = sb("statA", [1, 1024])
        statB = sb("statB", [1, 1024])
        rstdB = sb("rstdB", [128, T])
        mrsB = sb("mrsB", [128, T])
        xln = sb("xln", [128, KB * T], BF16)
        ones_col = sb("ones_col", [128, 1])
        ones_row = sb("ones_row", [1, 128])
        epsc = sb("epsc", [1, 1])
        ident = sb("ident", [128, 128])
        rwtb = sb("rwtb", [128, KB * NE], BF16)
        lg = sb("lg", [8, T])
        lgtm = sb("lgtm", [128, 64])
        t8 = sb("t8", [128, 32])
        e8 = sb("e8", [128, 8 * 4])
        combtm = sb("combtm", [128, 64])
        combfm = sb("combfm", [8, T], BF16)
        oh = sb("oh", [8, NE * 128], BF16)
        combB = sb("combB", [128, T], BF16)
        wg_st = sb("wg_st", [128, KB * 128])
        wu_st = sb("wu_st", [128, KB * 128])
        wd_st = sb("wd_st", [128, MBS * 128])
        wgb = sb("wgb", [128, KB * 128], BF16)
        wub = sb("wub", [128, KB * 128], BF16)
        wdb = sb("wdb", [128, MBS * 128], BF16)
        sg = sb("sg", [128, 1024], BF16)
        ub = sb("ub", [128, 1024], BF16)
        G = sb("G", [128, MBI * 1024], BF16)
        acc = sb("acc", [128, MBD * 1024])
        sgbT = sb("sgbT", [128, MBS])
        subT = sb("subT", [128, MBS])
        sdbT = sb("sdbT", [128, MBD])
        pbank = [ps(f"p{i}", [128, 512]) for i in range(8)]

        P.on("vector", lambda e: e.memset(ones_col[:], 1.0))
        P.on("vector", lambda e: e.memset(epsc[:], 1e-5))
        P.on("vector", lambda e: e.memset(ones_row[:], 1.0))
        P.on("gpsimd", lambda e: e.memset(oh[:], 1.0))
        P.dma("sync", lambda e: e.dma_start(out=sgbT[:], in_=sgb_d[:, :]))
        P.dma("sync", lambda e: e.dma_start(out=subT[:], in_=sub_d[:, :]))
        P.dma("sync", lambda e: e.dma_start(out=sdbT[:], in_=sdb_d[:, :]))
        P.barrier()
        _mk_ident(P, ident)
        for ee in range(NE):
            P.on("gpsimd", lambda e, ee=ee: e.affine_select(
                out=oh[:, ee * 128:(ee + 1) * 128], in_=oh[:, ee * 128:(ee + 1) * 128],
                compare_op=ALU.is_equal, fill=0.0, base=-ee,
                pattern=[[0, 128]], channel_multiplier=1))
        P.barrier()
        emit_ln_fm(P, nc, xt_d, xst, sqt, statA, statB, rstdB, mrsB, xln, ones_col,
                   ones_row, pbank, epsc)

        P.dma("sync", lambda e: e.dma_start(out=wg_st[:, 0:KB * NE], in_=rwt_d.rearrange(
            "(kb kp) m -> kp kb m", kp=128)))
        P.barrier()
        P.on("scalar", lambda e: e.activation(out=rwtb[:], in_=wg_st[:, 0:KB * NE],
                                              func=AF.Copy))
        P.barrier()
        for h in range(2):
            for kb in range(KB):
                P.on("tensor", lambda e, h=h, kb=kb: e.matmul(
                    pbank[h][0:8, :], rwtb[:, kb * 8:(kb + 1) * 8],
                    xln[:, kb * T + h * 512:kb * T + h * 512 + 512],
                    start=(kb == 0), stop=(kb == KB - 1)))
        P.barrier()
        for h in range(2):
            P.on("vector", lambda e, h=h: e.tensor_copy(lg[:, h * 512:h * 512 + 512],
                                                        pbank[h][0:8, :]))
        P.barrier()
        for tb in range(8):
            P.on("tensor", lambda e, tb=tb: e.transpose(
                pbank[2][:, tb * 8:(tb + 1) * 8], lg[0:8, tb * 128:(tb + 1) * 128],
                ident[0:8, 0:8]))
        P.barrier()
        P.on("vector", lambda e: e.tensor_copy(lgtm[:], pbank[2][:, 0:64]))
        P.barrier()
        for tb in range(8):
            P.on("vector", lambda e, tb=tb: e.tensor_reduce(
                t8[:, 3:4], lgtm[:, tb * 8:(tb + 1) * 8], axis=AX.X, op=ALU.max))
            P.barrier()
            P.on("vector", lambda e: e.tensor_single_scalar(
                out=t8[:, 3:4], in_=t8[:, 3:4], scalar=-1.0, op=ALU.mult))
            P.barrier()
            P.on("scalar", lambda e, tb=tb: e.activation(
                out=e8[:, 0:8], in_=lgtm[:, tb * 8:(tb + 1) * 8], func=AF.Exp,
                bias=t8[:, 3:4]))
            P.barrier()
            P.on("vector", lambda e: e.tensor_single_scalar(
                out=e8[:, 8:16], in_=e8[:, 0:8], scalar=1.0, op=ALU.is_ge))
            P.barrier()
            P.on("vector", lambda e: e.tensor_tensor(
                out=e8[:, 16:24], in0=e8[:, 0:8], in1=e8[:, 8:16], op=ALU.mult))
            P.barrier()
            P.on("vector", lambda e: e.tensor_tensor(
                out=e8[:, 16:24], in0=e8[:, 0:8], in1=e8[:, 16:24], op=ALU.subtract))
            P.barrier()
            P.on("vector", lambda e: e.tensor_reduce(
                t8[:, 0:1], e8[:, 16:24], axis=AX.X, op=ALU.max))
            P.barrier()
            P.on("vector", lambda e: e.tensor_scalar(
                out=e8[:, 24:32], in0=e8[:, 16:24], scalar1=t8[:, 0:1], scalar2=None,
                op0=ALU.is_ge))
            P.barrier()
            P.on("vector", lambda e: e.tensor_single_scalar(
                out=t8[:, 1:2], in_=t8[:, 0:1], scalar=1.0, op=ALU.add))
            P.barrier()
            P.on("vector", lambda e: e.reciprocal(t8[:, 2:3], t8[:, 1:2]))
            P.barrier()
            P.on("vector", lambda e: e.tensor_tensor(
                out=e8[:, 8:16], in0=e8[:, 8:16], in1=e8[:, 24:32], op=ALU.add))
            P.barrier()
            P.on("vector", lambda e: e.tensor_tensor(
                out=e8[:, 0:8], in0=e8[:, 0:8], in1=e8[:, 8:16], op=ALU.mult))
            P.barrier()
            P.on("vector", lambda e, tb=tb: e.tensor_scalar(
                out=combtm[:, tb * 8:(tb + 1) * 8], in0=e8[:, 0:8],
                scalar1=t8[:, 2:3], scalar2=None, op0=ALU.mult))
            P.barrier()
        for half in range(2):
            for tb in range(4):
                g = half * 4 + tb
                P.on("tensor", lambda e, g=g, tb=tb: e.transpose(
                    pbank[3][0:8, tb * 128:(tb + 1) * 128],
                    combtm[:, g * 8:(g + 1) * 8], ident[:]))
            P.barrier()
            P.on("vector", lambda e, half=half: e.tensor_copy(
                combfm[:, half * 512:half * 512 + 512], pbank[3][0:8, 0:512]))
            P.barrier()

        if dump:
            P.dma("gpsimd", lambda e: e.dma_start(out=dxln_d[:, :], in_=xln[:]))
            P.dma("gpsimd", lambda e: e.dma_start(out=dlg_d[:, :], in_=lg[:]))
            P.dma("gpsimd", lambda e: e.dma_start(out=dcomb_d[:, :], in_=combfm[:]))
            P.barrier()
        gcnt = []
        ucnt = []
        dcnt = []
        def issue_g(ee, mb):
            P.qwait("sync")
            gcnt.append(P.sdma("sync", "wg", lambda e, ee=ee, mb=mb: e.dma_start(
                out=wg_st[:], in_=gw_d[ee, mb])))
            P.qwait("gpsimd")
            ucnt.append(P.sdma("gpsimd", "wu", lambda e, ee=ee, mb=mb: e.dma_start(
                out=wu_st[:], in_=uw_d[ee, mb])))
        def issue_d(ee, mb):
            P.qwait("scalar")
            dcnt.append(P.sdma("scalar", "wd", lambda e, ee=ee, mb=mb: e.dma_start(
                out=wd_st[:, 0:MBI * 128], in_=dw_d[ee, mb])))

        issue_g(0, 0)
        gu_i = 0
        d_i = 0
        for ee in range(NE):
            for h in range(2):
                P.on("tensor", lambda e, ee=ee, h=h: e.matmul(
                    pbank[6][:], oh[0:8, ee * 128:(ee + 1) * 128],
                    combfm[0:8, h * 512:h * 512 + 512], start=True, stop=True))
                P.barrier()
                P.on("scalar", lambda e, h=h: e.activation(
                    out=combB[:, h * 512:h * 512 + 512], in_=pbank[6][:], func=AF.Copy))
                P.barrier()
            for mb in range(MBI):
                P.wait_stream("scalar", "wg", gcnt[gu_i])
                P.on("scalar", lambda e: e.activation(out=wgb[:], in_=wg_st[:],
                                                      func=AF.Copy))
                P.wait_stream("scalar", "wu", ucnt[gu_i])
                P.on("scalar", lambda e: e.activation(out=wub[:], in_=wu_st[:],
                                                      func=AF.Copy))
                P.barrier()
                gu_i += 1
                if mb + 1 < MBI:
                    issue_g(ee, mb + 1)
                elif ee + 1 < NE:
                    issue_g(ee + 1, 0)
                for h in range(2):
                    for kb in range(KB):
                        P.on("tensor", lambda e, h=h, kb=kb: e.matmul(
                            pbank[h][:], wgb[:, kb * 128:(kb + 1) * 128],
                            xln[:, kb * T + h * 512:kb * T + h * 512 + 512],
                            start=(kb == 0), stop=(kb == KB - 1)))
                    for kb in range(KB):
                        P.on("tensor", lambda e, h=h, kb=kb: e.matmul(
                            pbank[2 + h][:], wub[:, kb * 128:(kb + 1) * 128],
                            xln[:, kb * T + h * 512:kb * T + h * 512 + 512],
                            start=(kb == 0), stop=(kb == KB - 1)))
                P.barrier()
                for h in range(2):
                    P.on("scalar", lambda e, h=h: e.activation(
                        out=sg[:, h * 512:h * 512 + 512], in_=pbank[h][:],
                        func=AF.Silu))
                    P.on("vector", lambda e, h=h: e.tensor_tensor(
                        out=ub[:, h * 512:h * 512 + 512], in0=pbank[2 + h][:],
                        in1=combB[:, h * 512:h * 512 + 512], op=ALU.mult))
                P.barrier()
                P.on("vector", lambda e, mb=mb: e.tensor_tensor(
                    out=G[:, mb * 1024:(mb + 1) * 1024], in0=sg[:], in1=ub[:],
                    op=ALU.mult))
                P.barrier()
            issue_d(ee, 0)
            for mb in range(MBD):
                P.wait_stream("vector", "wd", dcnt[d_i])
                P.on("vector", lambda e: e.tensor_copy(wdb[:, 0:MBI * 128],
                                                       wd_st[:, 0:MBI * 128]))
                P.barrier()
                d_i += 1
                if mb + 1 < MBD:
                    issue_d(ee, mb + 1)
                for h in range(2):
                    for kb in range(MBI):
                        P.on("tensor", lambda e, h=h, kb=kb: e.matmul(
                            pbank[4 + h][:], wdb[:, kb * 128:(kb + 1) * 128],
                            G[:, kb * 1024 + h * 512:kb * 1024 + h * 512 + 512],
                            start=(kb == 0), stop=(kb == MBI - 1)))
                P.barrier()
                for h in range(2):
                    av = acc[:, mb * 1024 + h * 512:mb * 1024 + h * 512 + 512]
                    if ee == 0:
                        P.on("vector", lambda e, av=av, h=h: e.tensor_copy(
                            av, pbank[4 + h][:]))
                    else:
                        P.on("vector", lambda e, av=av, h=h: e.tensor_tensor(
                            out=av, in0=pbank[4 + h][:], in1=av, op=ALU.add))
                P.barrier()

        if dump:
            P.dma("gpsimd", lambda e: e.dma_start(out=dacc_d[:, :], in_=acc[:]))
            P.barrier()
        sgcnt = []
        sdcnt = []
        def issue_sg(mb):
            P.qwait("sync")
            sgcnt.append(P.sdma("sync", "wg", lambda e, mb=mb: e.dma_start(
                out=wg_st[:], in_=sgw_d[mb])))
            P.qwait("gpsimd")
            P.sdma("gpsimd", "wu", lambda e, mb=mb: e.dma_start(
                out=wu_st[:], in_=suw_d[mb]))
        def issue_sd(mb):
            P.qwait("scalar")
            sdcnt.append(P.sdma("scalar", "wd", lambda e, mb=mb: e.dma_start(
                out=wd_st[:], in_=sdw_d[mb])))
        for half in range(2):
            issue_sg(0)
            for mb in range(MBS):
                P.wait_stream("scalar", "wg", sgcnt[half * MBS + mb])
                P.on("scalar", lambda e: e.activation(out=wgb[:], in_=wg_st[:],
                                                      func=AF.Copy))
                P.wait_stream("scalar", "wu", None)
                P.on("scalar", lambda e: e.activation(out=wub[:], in_=wu_st[:],
                                                      func=AF.Copy))
                P.barrier()
                if mb + 1 < MBS:
                    issue_sg(mb + 1)
                for kb in range(KB):
                    P.on("tensor", lambda e, kb=kb, half=half: e.matmul(
                        pbank[0][:], wgb[:, kb * 128:(kb + 1) * 128],
                        xln[:, kb * T + half * 512:kb * T + half * 512 + 512],
                        start=(kb == 0), stop=(kb == KB - 1)))
                for kb in range(KB):
                    P.on("tensor", lambda e, kb=kb, half=half: e.matmul(
                        pbank[1][:], wub[:, kb * 128:(kb + 1) * 128],
                        xln[:, kb * T + half * 512:kb * T + half * 512 + 512],
                        start=(kb == 0), stop=(kb == KB - 1)))
                P.barrier()
                P.on("scalar", lambda e, mb=mb: e.activation(
                    out=sg[:, 0:512], in_=pbank[0][:], func=AF.Silu,
                    bias=sgbT[:, mb:mb + 1]))
                P.on("vector", lambda e, mb=mb: e.tensor_scalar(
                    out=ub[:, 0:512], in0=pbank[1][:], scalar1=subT[:, mb:mb + 1],
                    scalar2=None, op0=ALU.add))
                P.barrier()
                P.on("vector", lambda e, mb=mb: e.tensor_tensor(
                    out=G[:, mb * 512:mb * 512 + 512], in0=sg[:, 0:512],
                    in1=ub[:, 0:512], op=ALU.mult))
                P.barrier()
            if dump and half == 0:
                P.dma("gpsimd", lambda e: e.dma_start(out=dG_d[:, :], in_=G[:]))
                P.barrier()
            issue_sd(0)
            for mb in range(MBD):
                P.wait_stream("vector", "wd", sdcnt[half * MBD + mb])
                P.on("vector", lambda e: e.tensor_copy(wdb[:], wd_st[:]))
                P.barrier()
                if mb + 1 < MBD:
                    issue_sd(mb + 1)
                for kb in range(MBS):
                    P.on("tensor", lambda e, kb=kb: e.matmul(
                        pbank[4][:], wdb[:, kb * 128:(kb + 1) * 128],
                        G[:, kb * 512:kb * 512 + 512],
                        start=(kb == 0), stop=(kb == MBS - 1)))
                P.barrier()
                av = acc[:, mb * 1024 + half * 512:mb * 1024 + half * 512 + 512]
                P.on("vector", lambda e, av=av: e.tensor_tensor(
                    out=av, in0=pbank[4][:], in1=av, op=ALU.add))
                P.barrier()

        if dump:
            P.dma("gpsimd", lambda e: e.dma_start(out=dacc2_d[:, :], in_=acc[:]))
            P.barrier()
        fcnt = []
        for ch in range(2):
            mb, h = divmod(ch, 2)
            fcnt.append(P.sdma("sync", "wx", lambda e, mb=mb, h=h, ch=ch: e.dma_start(
                out=xst[:, (ch % 2) * 512:(ch % 2) * 512 + 512],
                in_=xt_d[mb * 128:(mb + 1) * 128, h * 512:(h + 1) * 512])))
        for ch in range(2 * MBD):
            mb, h = divmod(ch, 2)
            slot = ch % 2
            av = acc[:, mb * 1024 + h * 512:mb * 1024 + h * 512 + 512]
            P.on("vector", lambda e, av=av, mb=mb, slot=slot: e.tensor_scalar(
                out=sqt[:, slot * 512:slot * 512 + 512],
                in0=av, scalar1=sdbT[:, mb:mb + 1], scalar2=None, op0=ALU.add))
            P.wait_stream("vector", "wx", fcnt[ch])
            P.barrier()
            P.on("vector", lambda e, slot=slot: e.scalar_tensor_tensor(
                out=wg_st[:, slot * 512:slot * 512 + 512],
                in0=sqt[:, slot * 512:slot * 512 + 512], scalar=0.5,
                in1=xst[:, slot * 512:slot * 512 + 512], op0=ALU.mult, op1=ALU.add))
            P.barrier()
            P.dma("gpsimd", lambda e, mb=mb, h=h, slot=slot: e.dma_start(
                out=out_d[mb * 128:(mb + 1) * 128, h * 512:(h + 1) * 512],
                in_=wg_st[:, slot * 512:slot * 512 + 512]))
            if ch + 2 < 2 * MBD:
                mb2, h2 = divmod(ch + 2, 2)
                P.qwait("sync")
                fcnt.append(P.sdma("sync", "wx", lambda e, mb2=mb2, h2=h2, slot=slot:
                    e.dma_start(out=xst[:, slot * 512:slot * 512 + 512],
                                in_=xt_d[mb2 * 128:(mb2 + 1) * 128,
                                         h2 * 512:(h2 + 1) * 512])))
            P.barrier()
        P.on("gpsimd", lambda e: e.engine_nop())
        P.build(nc)
    return nc


def prep_l0ffn_weights(inp):
    gw = np.ascontiguousarray(inp["L0_ffn_gate_w"].reshape(8, 16, 128, 11, 128)
                              .transpose(0, 3, 2, 1, 4))
    uw = np.ascontiguousarray(inp["L0_ffn_up_w"].reshape(8, 16, 128, 11, 128)
                              .transpose(0, 3, 2, 1, 4))
    dw = np.ascontiguousarray(inp["L0_ffn_down_w"].reshape(8, 11, 128, 16, 128)
                              .transpose(0, 3, 2, 1, 4))
    sgw = np.ascontiguousarray(inp["L0_ffn_s_gate_w"].reshape(16, 128, 22, 128)
                               .transpose(2, 1, 0, 3))
    suw = np.ascontiguousarray(inp["L0_ffn_s_up_w"].reshape(16, 128, 22, 128)
                               .transpose(2, 1, 0, 3))
    sdw = np.ascontiguousarray(inp["L0_ffn_s_down_w"].reshape(22, 128, 16, 128)
                               .transpose(2, 1, 0, 3))
    sgb = np.ascontiguousarray(inp["L0_ffn_s_gate_b"].reshape(22, 128).T)
    sub = np.ascontiguousarray(inp["L0_ffn_s_up_b"].reshape(22, 128).T)
    sdb = np.ascontiguousarray(inp["L0_ffn_s_down_b"].reshape(16, 128).T)
    rwt = np.ascontiguousarray(inp["L0_ffn_router_w"].T)
    return dict(gw=gw, uw=uw, dw=dw, sgw=sgw, suw=suw, sdw=sdw, sgb=sgb, sub=sub,
                sdb=sdb, rwt=rwt)


def run_l0ffn(nc, xT, w):
    in_maps = []
    for c in range(NCORE):
        m = dict(w)
        m["xt"] = np.ascontiguousarray(xT[:, c * 1024:(c + 1) * 1024])
        in_maps.append(m)
    res = run_bass_kernel_spmd(nc, in_maps, core_ids=list(range(NCORE)))
    return np.concatenate([np.asarray(r["out"]) for r in res.results], axis=1)


def kernel(**inputs):
    x = inputs["L0_attn_x"].astype(np.float32)
    xT = np.ascontiguousarray(x.reshape(8192, 2048).T)
    w0 = prep_l0ffn_weights(inputs)
    nc_ffn = build_l0ffn()
    xT = run_l0ffn(nc_ffn, xT, w0)
    out = np.ascontiguousarray(xT.T).reshape(2, 4096, 2048)
    return out



def build_m1a():
    nc = bass.Bass("TRN2", target_bir_lowering=False)
    T, KB = 1536, 16
    xt_d = nc.dram_tensor("xt", [2048, T], F32, kind="ExternalInput").ap()
    wbc_d = nc.dram_tensor("wbc", [33, 128, KB, 128], F32, kind="ExternalInput").ap()
    wdt_d = nc.dram_tensor("wdt", [1, 128, KB, 32], F32, kind="ExternalInput").ap()
    cw_d = nc.dram_tensor("cw", [4224, 4], F32, kind="ExternalInput").ap()
    cb_d = nc.dram_tensor("cb", [128, 33], F32, kind="ExternalInput").ap()
    xc_d = nc.dram_tensor("xc", [4224, 1024], BF16, kind="ExternalOutput").ap()
    dtr_d = nc.dram_tensor("dtr", [32, 1024], F32, kind="ExternalOutput").ap()
    P = Prog()
    with contextlib.ExitStack() as stack:
        sb = lambda n, s, dt=F32: stack.enter_context(nc.sbuf_tensor(n, s, dt))
        ps = lambda n, s: stack.enter_context(nc.psum_tensor(n, s, F32))
        xst = sb("xst", [128, 1024])
        sqt = sb("sqt", [128, 1024])
        statA = sb("statA", [1, T])
        statB = sb("statB", [1, T])
        rstdB = sb("rstdB", [128, T])
        mrsB = sb("mrsB", [128, T])
        xln = sb("xln", [128, KB * T], BF16)
        ones_col = sb("ones_col", [128, 1])
        ones_row = sb("ones_row", [1, 128])
        epsc = sb("epsc", [1, 1])
        w_st = sb("w_st", [128, KB * 128])
        w_bf = sb("w_bf", [128, KB * 128], BF16)
        raw = sb("raw", [128, T])
        cacc = sb("cacc", [128, 1024])
        cwt = sb("cwt", [128, 4])
        cbt = sb("cbt", [128, 33])
        xcb = sb("xcb", [128, 1024], BF16)
        pbank = [ps(f"p{i}", [128, 512]) for i in range(8)]
        P.on("vector", lambda e: e.memset(ones_col[:], 1.0))
        P.on("vector", lambda e: e.memset(ones_row[:], 1.0))
        P.on("vector", lambda e: e.memset(epsc[:], 1e-5))
        P.dma("sync", lambda e: e.dma_start(out=cbt[:], in_=cb_d[:, :]))
        P.barrier()
        emit_ln_fm(P, nc, xt_d, xst, sqt, statA, statB, rstdB, mrsB, xln, ones_col,
                   ones_row, pbank, epsc, T=T, KB=KB)
        wcnt = []
        def issue_w(mb):
            P.qwait("sync")
            if mb < 33:
                wcnt.append(P.sdma("sync", "wb", lambda e, mb=mb: e.dma_start(
                    out=w_st[:], in_=wbc_d[mb])))
            else:
                wcnt.append(P.sdma("sync", "wb", lambda e: e.dma_start(
                    out=w_st[:, 0:KB * 32], in_=wdt_d[0])))
        issue_w(0)
        for mb in range(34):
            mwid = 128 if mb < 33 else 32
            P.wait_stream("scalar", "wb", wcnt[mb])
            P.on("scalar", lambda e, mwid=mwid: e.activation(
                out=w_bf[:, 0:KB * mwid], in_=w_st[:, 0:KB * mwid], func=AF.Copy))
            P.barrier()
            if mb + 1 < 34:
                issue_w(mb + 1)
            for ncn in range(3):
                for kb in range(KB):
                    P.on("tensor", lambda e, ncn=ncn, kb=kb, mwid=mwid: e.matmul(
                        pbank[ncn][0:mwid, :], w_bf[:, kb * mwid:(kb + 1) * mwid],
                        xln[:, kb * T + ncn * 512:kb * T + ncn * 512 + 512],
                        start=(kb == 0), stop=(kb == KB - 1)))
            P.barrier()
            if mb < 33:
                P.dma("sync", lambda e, mb=mb: e.dma_start(
                    out=cwt[:], in_=cw_d[mb * 128:(mb + 1) * 128, :]))
                for ncn in range(3):
                    P.on("vector", lambda e, ncn=ncn: e.tensor_copy(
                        raw[:, ncn * 512:ncn * 512 + 512], pbank[ncn][:]))
                P.barrier()
                P.on("vector", lambda e: e.tensor_scalar(
                    out=cacc[:], in0=raw[:, 509:1533], scalar1=cwt[:, 0:1],
                    scalar2=None, op0=ALU.mult))
                P.barrier()
                for k in range(1, 4):
                    P.on("vector", lambda e, k=k: e.scalar_tensor_tensor(
                        out=cacc[:], in0=raw[:, 509 + k:509 + k + 1024],
                        scalar=cwt[:, k:k + 1], in1=cacc[:], op0=ALU.mult,
                        op1=ALU.add))
                    P.barrier()
                P.on("scalar", lambda e, mb=mb: e.activation(
                    out=xcb[:], in_=cacc[:], func=AF.Silu, bias=cbt[:, mb:mb + 1]))
                P.barrier()
                P.dma("gpsimd", lambda e, mb=mb: e.dma_start(
                    out=xc_d[mb * 128:(mb + 1) * 128, :], in_=xcb[:]))
                P.barrier()
            else:
                for ncn in range(1, 3):
                    P.on("vector", lambda e, ncn=ncn: e.tensor_copy(
                        raw[0:32, (ncn - 1) * 512:(ncn - 1) * 512 + 512],
                        pbank[ncn][0:32, 0:512]))
                P.barrier()
                P.dma("gpsimd", lambda e: e.dma_start(out=dtr_d[:, :],
                                                      in_=raw[0:32, 0:1024]))
                P.barrier()
        P.on("gpsimd", lambda e: e.engine_nop())
        P.build(nc)
    return nc


def build_m1b():
    nc = bass.Bass("TRN2", target_bir_lowering=False)
    NCH = 32
    xch_d = nc.dram_tensor("xch", [1024, 4096], BF16, kind="ExternalInput").ap()
    bt_d = nc.dram_tensor("bt", [64, 4096], BF16, kind="ExternalInput").ap()
    ct_d = nc.dram_tensor("ct", [64, 4096], BF16, kind="ExternalInput").ap()
    dtr_d = nc.dram_tensor("dtr", [8, 4096], F32, kind="ExternalInput").ap()
    alog_d = nc.dram_tensor("alog", [1, 8], F32, kind="ExternalInput").ap()
    dtb_d = nc.dram_tensor("dtb", [1, 8], F32, kind="ExternalInput").ap()
    dd_d = nc.dram_tensor("dd", [1, 8], F32, kind="ExternalInput").ap()
    y_d = nc.dram_tensor("y", [4096, 1024], BF16, kind="ExternalOutput").ap()
    P = Prog()
    with contextlib.ExitStack() as stack:
        sb = lambda n, s, dt=F32: stack.enter_context(nc.sbuf_tensor(n, s, dt))
        B_t = sb("B_t", [64, 4096], BF16)
        C_t = sb("C_t", [64, 4096], BF16)
        xh = sb("xh", [128, 4096], BF16)
        BTm = sb("BTm", [128, NCH * 64], BF16)
        identf = sb("identf", [128, 128])
        identb = sb("identb", [128, 128], BF16)
        ones_row = sb("ones_row", [1, 128])
        rowbuf = sb("rowbuf", [1, 128])
        dtc = sb("dtc", [32, 128])
        sp1 = sb("sp1", [32, 128])
        sp2 = sb("sp2", [32, 128])
        zz = sb("zz", [32, 128])
        da = sb("da", [32, 128])
        acs = sb("acs", [32, 128])
        eA = sb("eA", [32, 128])
        dec = sb("dec", [32, 128])
        acsT = sb("acsT", [128, 32])
        decT = sb("decT", [128, 32])
        eAT = sb("eAT", [128, 32])
        dtT = sb("dtT", [128, 32])
        cdB = sb("cdB", [64, 32])
        parB = sb("parB", [32, 8])
        DB = sb("DB", [128, 8])
        nA = sb("nA", [1, 8])
        state = sb("state", [64, 128])
        state_bf = sb("state_bf", [64, 128], BF16)
        xcT = sb("xcT", [128, 128], BF16)
        xdT = sb("xdT", [128, 128], BF16)
        bdec = sb("bdec", [128, 64], BF16)
        lt1 = sb("lt1", [128, 128])
        ltmask = sb("ltmask", [128, 128])
        mt = sb("mt", [128, 128], BF16)
        yd = sb("yd", [128, 128])
        yout = sb("yout", [128, 128], BF16)
        with contextlib.ExitStack() as pstack:
            pp = lambda n, s, dt=F32: pstack.enter_context(nc.psum_tensor(n, s, dt))
            pG = pp("pG", [128, 128])
            pY = pp("pY", [128, 128])
            pO = pp("pO", [128, 128])
            pS = pp("pS", [64, 128])
            pB = pp("pB", [128, 128])
            pTb = pp("pTb", [128, 128], BF16)
            pTf = pp("pTf", [128, 128])
            P.dma("sync", lambda e: e.dma_start(out=B_t[:], in_=bt_d[:, :]))
            P.dma("sync", lambda e: e.dma_start(out=C_t[:], in_=ct_d[:, :]))
            P.dma("sync", lambda e: e.dma_start(out=nA[:], in_=alog_d[:, :]))
            P.on("vector", lambda e: e.memset(ones_row[:], 1.0))
            P.on("vector", lambda e: e.memset(zz[:], 0.0))
            P.barrier()
            _mk_ident(P, identf)
            _mk_ident(P, identb)
            P.on("gpsimd", lambda e: e.memset(ltmask[:], 1.0))
            P.barrier()
            P.on("gpsimd", lambda e: e.affine_select(out=ltmask[:], in_=ltmask[:],
                    compare_op=ALU.is_ge, fill=0.0, base=0,
                    pattern=[[1, 128]], channel_multiplier=-1))
            P.barrier()
            P.on("scalar", lambda e: e.activation(out=nA[:], in_=nA[:], func=AF.Exp))
            P.barrier()
            P.on("vector", lambda e: e.tensor_single_scalar(out=nA[:], in_=nA[:],
                                                            scalar=-1.0, op=ALU.mult))
            P.barrier()
            P.on("tensor", lambda e: e.matmul(pB[0:32, 0:8], ones_row[0:1, 0:32],
                                              nA[:], start=True, stop=True))
            P.barrier()
            P.on("vector", lambda e: e.tensor_copy(parB[:], pB[0:32, 0:8]))
            P.barrier()
            P.dma("sync", lambda e: e.dma_start(out=nA[:], in_=dtb_d[:, :]))
            P.barrier()
            P.on("tensor", lambda e: e.matmul(pB[0:32, 0:8], ones_row[0:1, 0:32],
                                              nA[:], start=True, stop=True))
            P.barrier()
            dtbB = sb("dtbB", [32, 8])
            P.on("vector", lambda e: e.tensor_copy(dtbB[:], pB[0:32, 0:8]))
            P.barrier()
            P.dma("sync", lambda e: e.dma_start(out=nA[:], in_=dd_d[:, :]))
            P.barrier()
            P.on("tensor", lambda e: e.matmul(pB[:, 0:8], ones_row[:],
                                              nA[:], start=True, stop=True))
            P.barrier()
            P.on("vector", lambda e: e.tensor_copy(DB[:], pB[:, 0:8]))
            P.barrier()
            for c in range(NCH):
                P.on("tensor", lambda e, c=c: e.transpose(
                    pTb[:, 0:64], B_t[:, c * 128:(c + 1) * 128], identb[0:64, 0:64]))
                P.barrier()
                P.on("vector", lambda e, c=c: e.tensor_copy(
                    BTm[:, c * 64:(c + 1) * 64], pTb[:, 0:64]))
                P.barrier()
            for h in range(8):
                P.dma("sync", lambda e, h=h: e.dma_start(
                    out=xh[:], in_=xch_d[h * 128:(h + 1) * 128, :]))
                P.dma("sync", lambda e, h=h: e.dma_start(
                    out=dtc[:], in_=dtr_d[h:h + 1, :].rearrange("o (c q) -> (o c) q", q=128)))
                P.barrier()
                P.on("scalar", lambda e, h=h: e.activation(out=sp1[:], in_=dtc[:],
                        func=AF.Abs, bias=dtbB[:, h:h + 1]))
                P.on("scalar", lambda e, h=h: e.activation(out=sp2[:], in_=dtc[:],
                        func=AF.Relu, bias=dtbB[:, h:h + 1]))
                P.barrier()
                P.on("scalar", lambda e: e.activation(out=sp1[:], in_=sp1[:],
                        func=AF.Exp, scale=-1.0))
                P.barrier()
                P.on("scalar", lambda e: e.activation(out=sp1[:], in_=sp1[:],
                        func=AF.Ln, bias=1.0))
                P.barrier()
                P.on("vector", lambda e: e.tensor_tensor(out=dtc[:], in0=sp1[:],
                        in1=sp2[:], op=ALU.add))
                P.barrier()
                P.on("vector", lambda e, h=h: e.tensor_scalar(out=da[:], in0=dtc[:],
                        scalar1=parB[:, h:h + 1], scalar2=None, op0=ALU.mult))
                P.barrier()
                P.on("vector", lambda e: e.tensor_tensor_scan(out=acs[:], data0=da[:],
                        data1=zz[:], initial=0.0, op0=ALU.add, op1=ALU.add))
                P.barrier()
                P.on("scalar", lambda e: e.activation(out=eA[:], in_=acs[:], func=AF.Exp))
                P.on("vector", lambda e: e.tensor_scalar(out=dec[:], in0=acs[:],
                        scalar1=acs[:, 127:128], scalar2=None, op0=ALU.subtract))
                P.barrier()
                P.on("scalar", lambda e: e.activation(out=dec[:], in_=dec[:],
                        func=AF.Exp, scale=-1.0))
                P.barrier()
                for src, dst in ((acs, acsT), (dec, decT), (eA, eAT), (dtc, dtT)):
                    P.on("tensor", lambda e, src=src: e.transpose(
                        pTf[:, 0:32], src[:], identf[0:32, 0:32]))
                    P.barrier()
                    P.on("vector", lambda e, dst=dst: e.tensor_copy(dst[:], pTf[:, 0:32]))
                    P.barrier()
                P.on("tensor", lambda e: e.transpose(pTf[0:1, 0:32],
                        eA[:, 127:128], identf[0:32, 0:32]))
                P.barrier()
                P.on("vector", lambda e: e.tensor_copy(rowbuf[0:1, 0:32], pTf[0:1, 0:32]))
                P.barrier()
                P.on("tensor", lambda e: e.matmul(pB[0:64, 0:32], ones_row[0:1, 0:64],
                        rowbuf[0:1, 0:32], start=True, stop=True))
                P.barrier()
                P.on("vector", lambda e: e.tensor_copy(cdB[:], pB[0:64, 0:32]))
                P.barrier()
                P.on("vector", lambda e: e.memset(state[:], 0.0))
                P.barrier()
                for c in range(NCH):
                    sl = slice(c * 128, (c + 1) * 128)
                    P.on("tensor", lambda e, sl=sl: e.transpose(pTb[:], xh[:, sl],
                                                                identb[:]))
                    P.barrier()
                    P.on("vector", lambda e: e.tensor_copy(xcT[:], pTb[:]))
                    P.barrier()
                    P.on("vector", lambda e, c=c: e.tensor_scalar(out=xdT[:],
                            in0=xcT[:], scalar1=dtT[:, c:c + 1], scalar2=None,
                            op0=ALU.mult))
                    P.on("tensor", lambda e, sl=sl: e.matmul(pG[:], B_t[:, sl],
                            C_t[:, sl], start=True, stop=True))
                    P.dma("scalar", lambda e, c=c: e.dma_start(
                        out=rowbuf[0:1, 0:128], in_=acs[c:c + 1, :]))
                    P.barrier()
                    P.on("tensor", lambda e: e.matmul(pB[:], ones_row[:],
                            rowbuf[0:1, 0:128], start=True, stop=True))
                    P.barrier()
                    P.on("vector", lambda e, c=c: e.tensor_scalar(out=lt1[:],
                            in0=pB[:], scalar1=acsT[:, c:c + 1], scalar2=None,
                            op0=ALU.subtract))
                    P.barrier()
                    P.on("vector", lambda e: e.tensor_single_scalar(out=lt1[:],
                            in_=lt1[:], scalar=0.0, op=ALU.min))
                    P.barrier()
                    P.on("scalar", lambda e: e.activation(out=lt1[:], in_=lt1[:],
                            func=AF.Exp))
                    P.barrier()
                    P.on("vector", lambda e: e.tensor_tensor(out=lt1[:], in0=lt1[:],
                            in1=ltmask[:], op=ALU.mult))
                    P.barrier()
                    P.on("vector", lambda e: e.tensor_tensor(out=mt[:], in0=pG[:],
                            in1=lt1[:], op=ALU.mult))
                    P.barrier()
                    P.on("tensor", lambda e: e.matmul(pY[:], mt[:], xdT[:],
                                                      start=True, stop=True))
                    P.on("vector", lambda e, c=c: e.tensor_scalar(out=bdec[:],
                            in0=BTm[:, c * 64:(c + 1) * 64], scalar1=decT[:, c:c + 1],
                            scalar2=None, op0=ALU.mult))
                    P.on("scalar", lambda e: e.activation(out=state_bf[:], in_=state[:],
                            func=AF.Copy))
                    P.barrier()
                    P.on("tensor", lambda e: e.matmul(pS[:], bdec[:], xdT[:],
                                                      start=True, stop=True))
                    P.on("tensor", lambda e, sl=sl: e.matmul(pO[:], C_t[:, sl],
                            state_bf[:], start=True, stop=True))
                    P.barrier()
                    P.on("vector", lambda e: e.tensor_copy(yd[:], pY[:]))
                    P.barrier()
                    P.on("vector", lambda e, c=c: e.scalar_tensor_tensor(out=yd[:],
                            in0=pO[:], scalar=eAT[:, c:c + 1], in1=yd[:],
                            op0=ALU.mult, op1=ALU.add))
                    P.barrier()
                    P.on("vector", lambda e, h=h: e.scalar_tensor_tensor(out=yout[:],
                            in0=xcT[:], scalar=DB[:, h:h + 1], in1=yd[:],
                            op0=ALU.mult, op1=ALU.add))
                    P.on("vector", lambda e, c=c: e.scalar_tensor_tensor(out=state[:],
                            in0=state[:], scalar=cdB[:, c:c + 1], in1=pS[:],
                            op0=ALU.mult, op1=ALU.add))
                    P.barrier()
                    P.dma("gpsimd", lambda e, c=c, h=h: e.dma_start(
                        out=y_d[c * 128:(c + 1) * 128, h * 128:(h + 1) * 128],
                        in_=yout[:]))
                    P.barrier()
            P.on("gpsimd", lambda e: e.engine_nop())
            P.build(nc)
    return nc


def build_m2():
    nc = bass.Bass("TRN2", target_bir_lowering=False)
    T, KB = 1024, 16
    xt_d = nc.dram_tensor("xt", [2048, T], F32, kind="ExternalInput").ap()
    y_d = nc.dram_tensor("y", [T, 4096], BF16, kind="ExternalInput").ap()
    wz_d = nc.dram_tensor("wz", [8, 16, 128, 512], F32, kind="ExternalInput").ap()
    wo_d = nc.dram_tensor("wo", [16, 32, 128, 128], F32, kind="ExternalInput").ap()
    out_d = nc.dram_tensor("out", [2048, T], F32, kind="ExternalOutput").ap()
    zs_d = nc.dram_tensor("zs", [T, 4096], BF16, kind="Internal").ap()
    P = Prog()
    with contextlib.ExitStack() as stack:
        sb = lambda n, s, dt=F32: stack.enter_context(nc.sbuf_tensor(n, s, dt))
        ps = lambda n, s: stack.enter_context(nc.psum_tensor(n, s, F32))
        xst = sb("xst", [128, 1024])
        sqt = sb("sqt", [128, 1024])
        statA = sb("statA", [1, T])
        statB = sb("statB", [1, T])
        rstdB = sb("rstdB", [128, T])
        mrsB = sb("mrsB", [128, T])
        xln = sb("xln", [128, KB * T], BF16)
        ones_col = sb("ones_col", [128, 1])
        ones_row = sb("ones_row", [1, 128])
        epsc = sb("epsc", [1, 1])
        epsy = sb("epsy", [128, 1])
        w_st = sb("w_st", [128, 512])
        w_bf = sb("w_bf", [128, 512], BF16)
        w_st2 = sb("w_st2", [128, 128])
        w_bf2 = sb("w_bf2", [128, 128], BF16)
        yrow = sb("yrow", [128, 4096], BF16)
        ysq = sb("ysq", [128, 4096])
        yms = sb("yms", [128, 1])
        gT = sb("gT", [128, 32 * T], BF16)
        identf = sb("identf", [128, 128])
        zrow = sb("zrow", [128, 4096], BF16)
        obuf = sb("obuf", [128, 128])
        pbank = [ps(f"p{i}", [128, 512]) for i in range(8)]
        P.on("vector", lambda e: e.memset(ones_col[:], 1.0))
        P.on("vector", lambda e: e.memset(ones_row[:], 1.0))
        P.on("vector", lambda e: e.memset(epsc[:], 1e-5))
        P.on("vector", lambda e: e.memset(epsy[:], 1.1920929e-07))
        P.barrier()
        _mk_ident(P, identf)
        emit_ln_fm(P, nc, xt_d, xst, sqt, statA, statB, rstdB, mrsB, xln, ones_col,
                   ones_row, pbank, epsc, T=T, KB=KB)
        wcnt = []
        def issue_z(ncn, kb):
            P.qwait("sync")
            wcnt.append(P.sdma("sync", "wz", lambda e, ncn=ncn, kb=kb: e.dma_start(
                out=w_st[:], in_=wz_d[ncn, kb])))
        issue_z(0, 0)
        wi = 0
        for ncn in range(8):
            for kb in range(16):
                P.wait_stream("scalar", "wz", wcnt[wi])
                P.on("scalar", lambda e: e.activation(out=w_bf[:], in_=w_st[:],
                                                      func=AF.Copy))
                P.barrier()
                wi += 1
                if kb + 1 < 16:
                    issue_z(ncn, kb + 1)
                elif ncn + 1 < 8:
                    issue_z(ncn + 1, 0)
                for tb in range(8):
                    P.on("tensor", lambda e, tb=tb, kb=kb: e.matmul(
                        pbank[tb][:], xln[:, kb * T + tb * 128:kb * T + (tb + 1) * 128],
                        w_bf[:], start=(kb == 0), stop=(kb == 15)))
                P.barrier()
            for tb in range(8):
                P.on("scalar", lambda e, tb=tb: e.activation(
                    out=zrow[:, 0:512], in_=pbank[tb][:], func=AF.Silu))
                P.barrier()
                P.dma("gpsimd", lambda e, tb=tb, ncn=ncn: e.dma_start(
                    out=zs_d[tb * 128:(tb + 1) * 128, ncn * 512:(ncn + 1) * 512],
                    in_=zrow[:, 0:512]))
                P.barrier()
        for tb in range(8):
            P.dma("sync", lambda e, tb=tb: e.dma_start(
                out=yrow[:], in_=y_d[tb * 128:(tb + 1) * 128, :]))
            P.dma("sync", lambda e, tb=tb: e.dma_start(
                out=zrow[:], in_=zs_d[tb * 128:(tb + 1) * 128, :]))
            P.barrier()
            P.on("scalar", lambda e: e.activation(out=ysq[:], in_=yrow[:],
                                                  func=AF.Square))
            P.barrier()
            P.on("vector", lambda e: e.reduce_sum(yms[:], ysq[:], axis=AX.X))
            P.barrier()
            P.on("scalar", lambda e: e.activation(out=yms[:], in_=yms[:], func=AF.Copy,
                                                  scale=1.0 / 4096.0))
            P.barrier()
            P.on("scalar", lambda e: e.activation(out=yms[:], in_=yms[:], func=AF.Sqrt,
                                                  bias=epsy[:]))
            P.barrier()
            P.on("vector", lambda e: e.reciprocal(yms[:], yms[:]))
            P.barrier()
            P.on("vector", lambda e: e.tensor_scalar(out=ysq[:], in0=yrow[:],
                    scalar1=yms[:], scalar2=None, op0=ALU.mult))
            P.barrier()
            P.on("vector", lambda e: e.tensor_tensor(out=ysq[:], in0=ysq[:],
                    in1=zrow[:], op=ALU.mult))
            P.barrier()
            for kb2 in range(32):
                P.on("tensor", lambda e, kb2=kb2: e.transpose(
                    pbank[0][:, 0:128], ysq[:, kb2 * 128:(kb2 + 1) * 128], identf[:]))
                P.barrier()
                P.on("vector", lambda e, kb2=kb2, tb=tb: e.tensor_copy(
                    gT[:, kb2 * T + tb * 128:kb2 * T + (tb + 1) * 128],
                    pbank[0][:, 0:128]))
                P.barrier()
        wcnt2 = []
        def issue_o(mbo, kb2):
            P.qwait("sync")
            wcnt2.append(P.sdma("sync", "wo", lambda e, mbo=mbo, kb2=kb2: e.dma_start(
                out=w_st2[:], in_=wo_d[mbo, kb2])))
        issue_o(0, 0)
        wi2 = 0
        xcnt = []
        for mbo in range(16):
            for kb2 in range(32):
                P.wait_stream("scalar", "wo", wcnt2[wi2])
                P.on("scalar", lambda e: e.activation(out=w_bf2[:], in_=w_st2[:],
                                                      func=AF.Copy))
                P.barrier()
                wi2 += 1
                if kb2 + 1 < 32:
                    issue_o(mbo, kb2 + 1)
                elif mbo + 1 < 16:
                    issue_o(mbo + 1, 0)
                for tb in range(8):
                    P.on("tensor", lambda e, tb=tb, kb2=kb2: e.matmul(
                        pbank[tb][:, 0:128], w_bf2[:],
                        gT[:, kb2 * T + tb * 128:kb2 * T + (tb + 1) * 128],
                        start=(kb2 == 0), stop=(kb2 == 31)))
                P.barrier()
            P.qwait("gpsimd")
            xcnt.append(P.sdma("gpsimd", "wx2", lambda e, mbo=mbo: e.dma_start(
                out=xst[:, 0:1024], in_=xt_d[mbo * 128:(mbo + 1) * 128, :])))
            for tb in range(8):
                P.wait_stream("vector", "wx2", xcnt[mbo])
                P.on("vector", lambda e, tb=tb: e.tensor_tensor(out=obuf[:],
                        in0=pbank[tb][:, 0:128], in1=xst[:, tb * 128:(tb + 1) * 128],
                        op=ALU.add))
                P.barrier()
                P.dma("gpsimd", lambda e, mbo=mbo, tb=tb: e.dma_start(
                    out=out_d[mbo * 128:(mbo + 1) * 128, tb * 128:(tb + 1) * 128],
                    in_=obuf[:]))
                P.barrier()
        P.on("gpsimd", lambda e: e.engine_nop())
        P.build(nc)
    return nc


def prep_mamba_weights(inp):
    winT = np.ascontiguousarray(inp["L0_attn_W_in"].T)
    wbc = np.ascontiguousarray(winT[:, 4096:8320].reshape(16, 128, 33, 128)
                               .transpose(2, 1, 0, 3))
    wdt = np.ascontiguousarray(winT[:, 8320:8352].reshape(16, 128, 1, 32)
                               .transpose(2, 1, 0, 3))
    wz = np.ascontiguousarray(winT[:, 0:4096].reshape(16, 128, 8, 512)
                              .transpose(2, 0, 1, 3))
    woutT = np.ascontiguousarray(inp["L0_attn_W_out"].T)
    wo = np.ascontiguousarray(woutT.reshape(32, 128, 16, 128).transpose(2, 0, 1, 3))
    cb = np.ascontiguousarray(inp["L0_attn_conv_b"].reshape(33, 128).T)
    return dict(wbc=wbc, wdt=wdt, wz=wz, wo=wo, cw=inp["L0_attn_conv_w"], cb=cb,
                alog=inp["L0_attn_A_log"], dtb=inp["L0_attn_dt_bias"],
                dd=inp["L0_attn_D"])


def run_mamba(ncs, xT, w):
    nc1a, nc1b, nc2 = ncs
    in_maps = []
    for c in range(NCORE):
        b = c // 4
        s0 = (c % 4) * 1024
        xs = np.zeros((2048, 1536), np.float32)
        lo = max(0, s0 - 512)
        seg = xT[:, b * 4096 + lo: b * 4096 + s0 + 1024]
        xs[:, 1536 - seg.shape[1]:] = seg
        in_maps.append({"xt": np.ascontiguousarray(xs), "wbc": w["wbc"],
                        "wdt": w["wdt"], "cw": w["cw"], "cb": w["cb"]})
    res = run_bass_kernel_spmd(nc1a, in_maps, core_ids=list(range(NCORE)))
    xc = [np.asarray(r["xc"]) for r in res.results]
    dtr = [np.asarray(r["dtr"]) for r in res.results]
    xc_b = [np.concatenate(xc[4 * b:4 * b + 4], axis=1) for b in range(2)]
    dt_b = [np.concatenate(dtr[4 * b:4 * b + 4], axis=1) for b in range(2)]
    in_maps = []
    for c in range(NCORE):
        b = c // 4
        hg = c % 4
        in_maps.append({
            "xch": np.ascontiguousarray(xc_b[b][hg * 1024:(hg + 1) * 1024]),
            "bt": np.ascontiguousarray(xc_b[b][4096:4160]),
            "ct": np.ascontiguousarray(xc_b[b][4160:4224]),
            "dtr": np.ascontiguousarray(dt_b[b][hg * 8:(hg + 1) * 8]),
            "alog": np.ascontiguousarray(w["alog"][hg * 8:(hg + 1) * 8][None, :]),
            "dtb": np.ascontiguousarray(w["dtb"][hg * 8:(hg + 1) * 8][None, :]),
            "dd": np.ascontiguousarray(w["dd"][hg * 8:(hg + 1) * 8][None, :])})
    res = run_bass_kernel_spmd(nc1b, in_maps, core_ids=list(range(NCORE)))
    yy = [np.asarray(r["y"]) for r in res.results]
    y_b = [np.concatenate(yy[4 * b:4 * b + 4], axis=1) for b in range(2)]
    in_maps = []
    for c in range(NCORE):
        b = c // 4
        s0 = (c % 4) * 1024
        in_maps.append({
            "xt": np.ascontiguousarray(xT[:, c * 1024:(c + 1) * 1024]),
            "y": np.ascontiguousarray(y_b[b][s0:s0 + 1024]),
            "wz": w["wz"], "wo": w["wo"]})
    res = run_bass_kernel_spmd(nc2, in_maps, core_ids=list(range(NCORE)))
    return np.concatenate([np.asarray(r["out"]) for r in res.results], axis=1)



def build_f1():
    nc = bass.Bass("TRN2", target_bir_lowering=False)
    T, KB = 1024, 16
    xt_d = nc.dram_tensor("xt", [2048, T], F32, kind="ExternalInput").ap()
    wr_d = nc.dram_tensor("wr", [128, 16], F32, kind="ExternalInput").ap()
    xlnr_d = nc.dram_tensor("xlnr", [T, 2048], F32, kind="ExternalOutput").ap()
    lgr_d = nc.dram_tensor("lgr", [1, T], F32, kind="ExternalOutput").ap()
    P = Prog()
    with contextlib.ExitStack() as stack:
        sb = lambda n, s, dt=F32: stack.enter_context(nc.sbuf_tensor(n, s, dt))
        ps = lambda n, s: stack.enter_context(nc.psum_tensor(n, s, F32))
        xst = sb("xst", [128, 1024])
        sqt = sb("sqt", [128, 1024])
        statA = sb("statA", [1, T])
        statB = sb("statB", [1, T])
        rstdB = sb("rstdB", [128, T])
        mrsB = sb("mrsB", [128, T])
        xln = sb("xln", [128, T], BF16)
        xlf = sb("xlf", [128, T])
        wrt = sb("wrt", [128, 16])
        lrow = sb("lrow", [1, T])
        otile = sb("otile", [128, 128])
        identf = sb("identf", [128, 128])
        ones_col = sb("ones_col", [128, 1])
        ones_row = sb("ones_row", [1, 128])
        epsc = sb("epsc", [1, 1])
        pbank = [ps(f"p{i}", [128, 512]) for i in range(8)]
        P.on("vector", lambda e: e.memset(ones_col[:], 1.0))
        P.on("vector", lambda e: e.memset(ones_row[:], 1.0))
        P.on("vector", lambda e: e.memset(epsc[:], 1e-5))
        P.dma("sync", lambda e: e.dma_start(out=wrt[:], in_=wr_d[:, :]))
        P.barrier()
        _mk_ident(P, identf)
        emit_ln_fm(P, nc, xt_d, xst, sqt, statA, statB, rstdB, mrsB, xln, ones_col,
                   ones_row, pbank, epsc, T=T, KB=KB, apply=False)
        P.qwait("sync")
        cnts = []
        cnts.append(P.sdma("sync", "wx", lambda e: e.dma_start(
            out=xst[:], in_=xt_d[0:128, :])))
        for kb in range(KB):
            P.wait_stream("vector", "wx", cnts[kb])
            P.on("vector", lambda e: e.tensor_tensor(out=sqt[:], in0=xst[:],
                    in1=rstdB[:], op=ALU.mult))
            P.barrier()
            if kb + 1 < KB:
                P.qwait("sync")
                cnts.append(P.sdma("sync", "wx", lambda e, kb=kb: e.dma_start(
                    out=xst[:], in_=xt_d[(kb + 1) * 128:(kb + 2) * 128, :])))
            P.on("vector", lambda e: e.tensor_tensor(out=xlf[:], in0=sqt[:],
                    in1=mrsB[:], op=ALU.subtract))
            P.barrier()
            for h in range(2):
                P.on("tensor", lambda e, kb=kb, h=h: e.matmul(
                    pbank[4 + h][0:1, :], wrt[:, kb:kb + 1],
                    xlf[:, h * 512:h * 512 + 512],
                    start=(kb == 0), stop=(kb == KB - 1)))
            P.barrier()
            for tb in range(8):
                P.on("tensor", lambda e, tb=tb: e.transpose(
                    pbank[6][:, 0:128], xlf[:, tb * 128:(tb + 1) * 128], identf[:]))
                P.barrier()
                P.on("vector", lambda e: e.tensor_copy(otile[:], pbank[6][:, 0:128]))
                P.barrier()
                P.dma("gpsimd", lambda e, tb=tb, kb=kb: e.dma_start(
                    out=xlnr_d[tb * 128:(tb + 1) * 128, kb * 128:(kb + 1) * 128],
                    in_=otile[:]))
                P.barrier()
        for h in range(2):
            P.on("vector", lambda e, h=h: e.tensor_copy(
                lrow[:, h * 512:h * 512 + 512], pbank[4 + h][0:1, :]))
        P.barrier()
        P.dma("gpsimd", lambda e: e.dma_start(out=lgr_d[:, :], in_=lrow[:]))
        P.barrier()
        P.on("gpsimd", lambda e: e.engine_nop())
        P.build(nc)
    return nc


def build_f15():
    nc = bass.Bass("TRN2", target_bir_lowering=False)
    lg_d = nc.dram_tensor("lg", [1, 4096], F32, kind="ExternalInput").ap()
    xlnr_d = nc.dram_tensor("xlnr", [4096, 2048], F32, kind="ExternalInput").ap()
    lgc_d = nc.dram_tensor("lgc", [4096, 1], F32, kind="ExternalInput").ap()
    xf_d = nc.dram_tensor("xf", [1024, 2048], F32, kind="ExternalOutput").ap()
    selo_d = nc.dram_tensor("selo", [1024, 1], F32, kind="ExternalOutput").ap()
    rwo_d = nc.dram_tensor("rwo", [128, 8], F32, kind="ExternalOutput").ap()
    P = Prog()
    with contextlib.ExitStack() as stack:
        sb = lambda n, s, dt=F32: stack.enter_context(nc.sbuf_tensor(n, s, dt))
        ps = lambda n, s: stack.enter_context(nc.psum_tensor(n, s, F32))
        lgt = sb("lgt", [1, 4096])
        zz = sb("zz", [1, 4096])
        msk = sb("msk", [1, 4096])
        pos = sb("pos", [1, 4096])
        sc = sb("sc", [1, 16])
        offc = sb("offc", [128, 32])
        offi = sb("offi", [128, 32], I32)
        tokf = sb("tokf", [128, 32])
        selc = sb("selc", [128, 8])
        seli = sb("seli", [128, 8], I32)
        lsc = sb("lsc", [128, 8])
        ecol = sb("ecol", [128, 8])
        col1 = sb("col1", [128, 1])
        ones_row = sb("ones_row", [1, 128])
        ones_col = sb("ones_col", [128, 1])
        pb = ps("pb", [128, 512])
        lo = sc[0:1, 0:1]; hi = sc[0:1, 1:2]; mid = sc[0:1, 2:3]
        cnt = sc[0:1, 3:4]; cc = sc[0:1, 4:5]; t1 = sc[0:1, 5:6]; t2 = sc[0:1, 6:7]
        P.dma("sync", lambda e: e.dma_start(out=lgt[:], in_=lg_d[:, :]))
        P.on("vector", lambda e: e.memset(zz[:], 0.0))
        P.on("vector", lambda e: e.memset(ones_row[:], 1.0))
        P.on("vector", lambda e: e.memset(ones_col[:], 1.0))
        P.barrier()
        P.on("vector", lambda e: e.tensor_reduce(lo[:], lgt[:], axis=AX.X, op=ALU.min))
        P.on("vector", lambda e: e.tensor_reduce(hi[:], lgt[:], axis=AX.X, op=ALU.max))
        P.barrier()
        P.on("vector", lambda e: e.tensor_single_scalar(out=lo[:], in_=lo[:],
                scalar=-1.0, op=ALU.add))
        P.barrier()
        for it in range(42):
            P.on("vector", lambda e: e.tensor_tensor(out=mid[:], in0=lo[:], in1=hi[:],
                    op=ALU.add))
            P.barrier()
            P.on("vector", lambda e: e.tensor_single_scalar(out=mid[:], in_=mid[:],
                    scalar=0.5, op=ALU.mult))
            P.barrier()
            P.on("vector", lambda e: e.tensor_scalar(out=msk[:], in0=lgt[:],
                    scalar1=mid[:], scalar2=None, op0=ALU.is_ge))
            P.barrier()
            P.on("vector", lambda e: e.tensor_reduce(cnt[:], msk[:], axis=AX.X,
                    op=ALU.add))
            P.barrier()
            P.on("vector", lambda e: e.tensor_single_scalar(out=cc[:], in_=cnt[:],
                    scalar=1024.0, op=ALU.is_ge))
            P.barrier()
            P.on("vector", lambda e: e.tensor_tensor(out=t1[:], in0=mid[:], in1=lo[:],
                    op=ALU.subtract))
            P.on("vector", lambda e: e.tensor_tensor(out=t2[:], in0=hi[:], in1=mid[:],
                    op=ALU.subtract))
            P.barrier()
            P.on("vector", lambda e: e.scalar_tensor_tensor(out=lo[:], in0=t1[:],
                    scalar=cc[:], in1=lo[:], op0=ALU.mult, op1=ALU.add))
            P.on("vector", lambda e: e.scalar_tensor_tensor(out=t2[:], in0=t2[:],
                    scalar=cc[:], in1=mid[:], op0=ALU.mult, op1=ALU.add))
            P.barrier()
            P.on("vector", lambda e: e.tensor_copy(hi[:], t2[:]))
            P.barrier()
        P.on("vector", lambda e: e.tensor_scalar(out=msk[:], in0=lgt[:],
                scalar1=lo[:], scalar2=None, op0=ALU.is_ge))
        P.barrier()
        P.on("vector", lambda e: e.tensor_tensor_scan(out=pos[:], data0=msk[:],
                data1=zz[:], initial=0.0, op0=ALU.add, op1=ALU.add))
        P.barrier()
        P.on("vector", lambda e: e.tensor_single_scalar(out=pos[:], in_=pos[:],
                scalar=-1.0, op=ALU.add))
        P.barrier()
        P.on("vector", lambda e: e.tensor_single_scalar(out=msk[:], in_=msk[:],
                scalar=-1.0, op=ALU.add))
        P.barrier()
        P.on("vector", lambda e: e.scalar_tensor_tensor(out=pos[:], in0=msk[:],
                scalar=-2000000.0, in1=pos[:], op0=ALU.mult, op1=ALU.add))
        P.barrier()
        posd_d = nc.dram_tensor("posd", [1, 4096], F32, kind="Internal").ap()
        P.dma("sync", lambda e: e.dma_start(out=posd_d[:, :], in_=pos[:]))
        P.barrier()
        P.dma("sync", lambda e: e.dma_start(out=offc[:],
                in_=posd_d.rearrange("o (c p) -> (o p) c", p=128),
                allow_slow_non_contiguous=True))
        P.barrier()
        P.on("vector", lambda e: e.tensor_copy(offi[:], offc[:]))
        P.on("gpsimd", lambda e: e.iota(out=tokf[:], pattern=[[128, 32]], base=0,
                channel_multiplier=1, allow_small_or_imprecise_dtypes=True))
        P.barrier()
        for ch in range(32):
            P.dma("gpsimd", lambda e, ch=ch: e.indirect_dma_start(
                out=selo_d[:, :], out_offset=IndirectOffsetOnAxis(offi[:, ch:ch + 1], 0),
                in_=tokf[:, ch:ch + 1], in_offset=None,
                bounds_check=1023, oob_is_err=False))
        P.barrier()
        xfsb = sb("xfsb", [128, 2048])
        lg1 = sb("lg1", [128, 1])
        P.dma("sync", lambda e: e.dma_start(out=selc[:],
                in_=selo_d.rearrange("(c p) o -> (o p) c", p=128),
                allow_slow_non_contiguous=True))
        P.barrier()
        P.on("vector", lambda e: e.tensor_copy(seli[:], selc[:]))
        P.barrier()
        for ch in range(8):
            P.dma("gpsimd", lambda e, ch=ch: e.indirect_dma_start(
                out=xfsb[:], out_offset=None,
                in_=xlnr_d[:, :], in_offset=IndirectOffsetOnAxis(seli[:, ch:ch + 1], 0)))
            P.dma("gpsimd", lambda e, ch=ch: e.indirect_dma_start(
                out=lg1[:], out_offset=None,
                in_=lgc_d[:, :], in_offset=IndirectOffsetOnAxis(seli[:, ch:ch + 1], 0)))
            P.barrier()
            P.on("vector", lambda e, ch=ch: e.tensor_copy(lsc[:, ch:ch + 1], lg1[:]))
            P.barrier()
            P.dma("sync", lambda e, ch=ch: e.dma_start(
                out=xf_d[ch * 128:(ch + 1) * 128, :], in_=xfsb[:]))
            P.barrier()
        P.on("vector", lambda e: e.tensor_reduce(col1[:], lsc[:], axis=AX.X, op=ALU.max))
        P.barrier()
        P.on("gpsimd", lambda e: e.tensor_reduce(sc[0:1, 7:8], col1[:], axis=AX.C,
                op=ALU.max))
        P.barrier()
        P.on("tensor", lambda e: e.matmul(pb[:, 0:1], ones_row[:], sc[0:1, 7:8],
                start=True, stop=True))
        P.barrier()
        P.on("vector", lambda e: e.tensor_copy(col1[:], pb[:, 0:1]))
        P.barrier()
        P.on("vector", lambda e: e.tensor_scalar(out=ecol[:], in0=lsc[:],
                scalar1=col1[:], scalar2=None, op0=ALU.subtract))
        P.barrier()
        P.on("scalar", lambda e: e.activation(out=ecol[:], in_=ecol[:], func=AF.Exp))
        P.barrier()
        P.on("vector", lambda e: e.tensor_reduce(col1[:], ecol[:], axis=AX.X,
                op=ALU.add))
        P.barrier()
        P.on("tensor", lambda e: e.matmul(pb[0:1, 1:2], ones_col[:], col1[:],
                start=True, stop=True))
        P.barrier()
        P.on("vector", lambda e: e.reciprocal(sc[0:1, 8:9], pb[0:1, 1:2]))
        P.barrier()
        P.on("tensor", lambda e: e.matmul(pb[:, 2:3], ones_row[:], sc[0:1, 8:9],
                start=True, stop=True))
        P.barrier()
        P.on("vector", lambda e: e.tensor_copy(col1[:], pb[:, 2:3]))
        P.barrier()
        P.on("vector", lambda e: e.tensor_scalar(out=ecol[:], in0=ecol[:],
                scalar1=col1[:], scalar2=None, op0=ALU.mult))
        P.barrier()
        P.dma("gpsimd", lambda e: e.dma_start(out=rwo_d[:, :], in_=ecol[:]))
        P.barrier()
        P.on("gpsimd", lambda e: e.engine_nop())
        P.build(nc)
    return nc


def build_f2a():
    nc = bass.Bass("TRN2", target_bir_lowering=False)
    TA, KB = 1024, 16
    xft_d = nc.dram_tensor("xft", [2048, TA], F32, kind="ExternalInput").ap()
    maskt_d = nc.dram_tensor("maskt", [TA, TA], F32, kind="ExternalInput").ap()
    ct_d = nc.dram_tensor("ct", [64, TA], F32, kind="ExternalInput").ap()
    st_d = nc.dram_tensor("st", [64, TA], F32, kind="ExternalInput").ap()
    g1_d = nc.dram_tensor("g1", [128, 16], F32, kind="ExternalInput").ap()
    wq_d = nc.dram_tensor("wq", [4, 128, 16, 128], F32, kind="ExternalInput").ap()
    wk_d = nc.dram_tensor("wk", [4, 128, 16, 128], F32, kind="ExternalInput").ap()
    wv_d = nc.dram_tensor("wv", [4, 128, 16, 128], F32, kind="ExternalInput").ap()
    o_d = nc.dram_tensor("o", [512, TA], BF16, kind="ExternalOutput").ap()
    P = Prog()
    with contextlib.ExitStack() as stack:
        sb = lambda n, s, dt=F32: stack.enter_context(nc.sbuf_tensor(n, s, dt))
        xst = sb("xst", [128, 1024])
        sqt = sb("sqt", [128, 1024])
        hb = sb("hb", [128, KB * TA], BF16)
        rstdB = sb("rstdB", [128, TA])
        rst_row = sb("rst_row", [1, TA])
        g1c = sb("g1c", [128, 16])
        ctb = sb("ctb", [128, TA], BF16)
        stb = sb("stb", [128, TA], BF16)
        rA = sb("rA", [64, TA], BF16)
        rB = sb("rB", [64, TA], BF16)
        w_st = sb("w_st", [128, 2048])
        w_bf = sb("w_bf", [128, 2048], BF16)
        Qh = sb("Qh", [128, TA], BF16)
        Kh = sb("Kh", [128, TA], BF16)
        Vh = sb("Vh", [128, TA], BF16)
        VT = sb("VT", [128, 8 * 128], BF16)
        mk1 = sb("mk1", [128, TA])
        S2buf = sb("S2buf", [128, 8 * 512], BF16)
        mRb = sb("mRb", [128, 512], BF16)
        mrow = sb("mrow", [1, 512], BF16)
        nmB = sb("nmB", [128, 512], BF16)
        lrow = sb("lrow", [1, 512])
        rlB = sb("rlB", [128, 512])
        ob = sb("ob", [128, 512], BF16)
        identb = sb("identb", [128, 128], BF16)
        ones_col = sb("ones_col", [128, 1])
        ones_colb = sb("ones_colb", [128, 1], BF16)
        ones_row = sb("ones_row", [1, 128])
        ones_rowb = sb("ones_rowb", [1, 128], BF16)
        epsc = sb("epsc", [1, 1])
        with contextlib.ExitStack() as pstack:
            pp = lambda n, s, dt=F32: pstack.enter_context(nc.psum_tensor(n, s, dt))
            pb0 = pp("pb0", [128, 512])
            pb1 = pp("pb1", [128, 512])
            pS = pp("pS", [128, 512])
            pO = pp("pO", [128, 512])
            pl = pp("pl", [1, 512])
            pbc = pp("pbc", [128, 512])
            pTb = pp("pTb", [128, 128], BF16)
            P.on("vector", lambda e: e.memset(ones_col[:], 1.0))
            P.on("vector", lambda e: e.memset(ones_colb[:], 1.0))
            P.on("vector", lambda e: e.memset(ones_row[:], 1.0))
            P.on("vector", lambda e: e.memset(ones_rowb[:], 1.0))
            P.on("vector", lambda e: e.memset(epsc[:], 1e-6))
            P.dma("sync", lambda e: e.dma_start(out=g1c[:], in_=g1_d[:, :]))
            P.dma("sync", lambda e: e.dma_start(out=mk1[0:64, 0:TA], in_=ct_d[:, :]))
            P.barrier()
            P.on("scalar", lambda e: e.activation(out=ctb[0:64, :], in_=mk1[0:64, 0:TA],
                                                  func=AF.Copy))
            P.on("scalar", lambda e: e.activation(out=ctb[64:128, :], in_=mk1[0:64, 0:TA],
                                                  func=AF.Copy))
            P.barrier()
            P.dma("sync", lambda e: e.dma_start(out=mk1[0:64, 0:TA], in_=st_d[:, :]))
            P.barrier()
            P.on("scalar", lambda e: e.activation(out=stb[0:64, :], in_=mk1[0:64, 0:TA],
                                                  func=AF.Copy))
            P.on("scalar", lambda e: e.activation(out=stb[64:128, :], in_=mk1[0:64, 0:TA],
                                                  func=AF.Copy))
            P.barrier()
            _mk_ident(P, identb)
            cnts = []
            for ch in range(2):
                kb, h = divmod(ch, 2)
                cnts.append(P.sdma("sync", "wx", lambda e, kb=kb, h=h, ch=ch:
                    e.dma_start(out=xst[:, (ch % 2) * 512:(ch % 2) * 512 + 512],
                                in_=xft_d[kb * 128:(kb + 1) * 128,
                                          h * 512:(h + 1) * 512])))
            for ch in range(32):
                kb, h = divmod(ch, 2)
                slot = ch % 2
                P.wait_stream("scalar", "wx", cnts[ch])
                P.on("scalar", lambda e, slot=slot: e.activation(
                    out=sqt[:, slot * 512:slot * 512 + 512],
                    in_=xst[:, slot * 512:slot * 512 + 512], func=AF.Square))
                P.barrier()
                P.on("tensor", lambda e, h=h, slot=slot, kb=kb: e.matmul(
                    (pb0 if h == 0 else pb1)[0:1, :], ones_col[:],
                    sqt[:, slot * 512:slot * 512 + 512],
                    start=(kb == 0), stop=(kb == 15)))
                P.barrier()
                if ch + 2 < 32:
                    kb2, h2 = divmod(ch + 2, 2)
                    P.qwait("sync")
                    cnts.append(P.sdma("sync", "wx", lambda e, kb2=kb2, h2=h2, slot=slot:
                        e.dma_start(out=xst[:, slot * 512:slot * 512 + 512],
                                    in_=xft_d[kb2 * 128:(kb2 + 1) * 128,
                                              h2 * 512:(h2 + 1) * 512])))
            for h in range(2):
                P.on("vector", lambda e, h=h: e.tensor_copy(
                    rst_row[:, h * 512:h * 512 + 512], (pb0 if h == 0 else pb1)[0:1, :]))
            P.barrier()
            P.on("scalar", lambda e: e.activation(out=rst_row[:], in_=rst_row[:],
                    func=AF.Copy, scale=1.0 / 2048.0))
            P.barrier()
            P.on("scalar", lambda e: e.activation(out=rst_row[:], in_=rst_row[:],
                    func=AF.Sqrt, bias=epsc[:]))
            P.barrier()
            P.on("vector", lambda e: e.reciprocal(rst_row[:], rst_row[:]))
            P.barrier()
            for h in range(2):
                P.on("tensor", lambda e, h=h: e.matmul((pb0 if h == 0 else pb1)[:],
                        ones_row[:], rst_row[:, h * 512:h * 512 + 512],
                        start=True, stop=True))
            P.barrier()
            for h in range(2):
                P.on("vector", lambda e, h=h: e.tensor_copy(
                    rstdB[:, h * 512:h * 512 + 512], (pb0 if h == 0 else pb1)[:]))
            P.barrier()
            P.qwait("sync")
            cnts2 = []
            for ch in range(2):
                kb, h = divmod(ch, 2)
                cnts2.append(P.sdma("sync", "wx", lambda e, kb=kb, h=h, ch=ch:
                    e.dma_start(out=xst[:, (ch % 2) * 512:(ch % 2) * 512 + 512],
                                in_=xft_d[kb * 128:(kb + 1) * 128,
                                          h * 512:(h + 1) * 512])))
            for ch in range(32):
                kb, h = divmod(ch, 2)
                slot = ch % 2
                P.wait_stream("vector", "wx", cnts2[ch])
                P.on("vector", lambda e, slot=slot, h=h: e.tensor_tensor(
                    out=sqt[:, slot * 512:slot * 512 + 512],
                    in0=xst[:, slot * 512:slot * 512 + 512],
                    in1=rstdB[:, h * 512:h * 512 + 512], op=ALU.mult))
                P.barrier()
                P.on("vector", lambda e, slot=slot, kb=kb, h=h: e.tensor_scalar(
                    out=hb[:, kb * TA + h * 512:kb * TA + h * 512 + 512],
                    in0=sqt[:, slot * 512:slot * 512 + 512],
                    scalar1=g1c[:, kb:kb + 1], scalar2=None, op0=ALU.mult))
                P.barrier()
                if ch + 2 < 32:
                    kb2, h2 = divmod(ch + 2, 2)
                    P.qwait("sync")
                    cnts2.append(P.sdma("sync", "wx", lambda e, kb2=kb2, h2=h2,
                        slot=slot: e.dma_start(
                            out=xst[:, slot * 512:slot * 512 + 512],
                            in_=xft_d[kb2 * 128:(kb2 + 1) * 128,
                                      h2 * 512:(h2 + 1) * 512])))
            for hh in range(4):
                for wname, wd, dst in (("q", wq_d, Qh), ("k", wk_d, Kh), ("v", wv_d, Vh)):
                    P.qwait("sync")
                    cw = P.sdma("sync", "ww", lambda e, wd=wd, hh=hh: e.dma_start(
                        out=w_st[:], in_=wd[hh]))
                    P.wait_stream("scalar", "ww", cw)
                    P.on("scalar", lambda e: e.activation(out=w_bf[:], in_=w_st[:],
                                                          func=AF.Copy))
                    P.barrier()
                    for h in range(2):
                        for kb in range(KB):
                            P.on("tensor", lambda e, h=h, kb=kb: e.matmul(
                                (pb0 if h == 0 else pb1)[:],
                                w_bf[:, kb * 128:(kb + 1) * 128],
                                hb[:, kb * TA + h * 512:kb * TA + h * 512 + 512],
                                start=(kb == 0), stop=(kb == KB - 1)))
                    P.barrier()
                    for h in range(2):
                        P.on("vector", lambda e, h=h, dst=dst: e.tensor_copy(
                            dst[:, h * 512:h * 512 + 512], (pb0 if h == 0 else pb1)[:]))
                    P.barrier()
                for t in (Qh, Kh):
                    P.on("vector", lambda e, t=t: e.tensor_tensor(out=rA[:],
                            in0=t[0:64, :], in1=ctb[0:64, :], op=ALU.mult))
                    P.on("vector", lambda e, t=t: e.tensor_tensor(out=rB[:],
                            in0=t[64:128, :], in1=stb[64:128, :], op=ALU.mult))
                    P.barrier()
                    P.on("vector", lambda e: e.tensor_tensor(out=rA[:], in0=rA[:],
                            in1=rB[:], op=ALU.subtract))
                    P.on("vector", lambda e, t=t: e.tensor_tensor(out=rB[:],
                            in0=t[0:64, :], in1=stb[0:64, :], op=ALU.mult))
                    P.barrier()
                    P.on("vector", lambda e, t=t: e.tensor_tensor(out=t[0:64, :],
                            in0=t[64:128, :], in1=ctb[64:128, :], op=ALU.mult))
                    P.barrier()
                    P.on("vector", lambda e, t=t: e.tensor_tensor(out=t[64:128, :],
                            in0=rB[:], in1=t[0:64, :], op=ALU.add))
                    P.barrier()
                    P.on("vector", lambda e, t=t: e.tensor_copy(t[0:64, :], rA[:]))
                    P.barrier()
                P.on("vector", lambda e: e.tensor_single_scalar(out=Qh[:], in_=Qh[:],
                        scalar=0.08838834764831845, op=ALU.mult))
                P.barrier()
                for tkb in range(8):
                    P.on("tensor", lambda e, tkb=tkb: e.transpose(
                        pTb[:], Vh[:, tkb * 128:(tkb + 1) * 128], identb[:]))
                    P.barrier()
                    P.on("vector", lambda e, tkb=tkb: e.tensor_copy(
                        VT[:, tkb * 128:(tkb + 1) * 128], pTb[:]))
                    P.barrier()
                for tqc in range(2):
                    P.on("vector", lambda e: e.memset(mRb[:], -1e38))
                    P.barrier()
                    for tkb in range(8):
                        P.dma("scalar", lambda e, tkb=tkb, tqc=tqc: e.dma_start(
                            out=mk1[:, 0:512],
                            in_=maskt_d[tkb * 128:(tkb + 1) * 128,
                                        tqc * 512:(tqc + 1) * 512]))
                        P.on("tensor", lambda e, tkb=tkb, tqc=tqc: e.matmul(
                            pS[:], Kh[:, tkb * 128:(tkb + 1) * 128],
                            Qh[:, tqc * 512:(tqc + 1) * 512], start=True, stop=True))
                        P.barrier()
                        P.on("vector", lambda e, tkb=tkb: e.tensor_tensor(
                            out=S2buf[:, tkb * 512:(tkb + 1) * 512], in0=pS[:],
                            in1=mk1[:, 0:512], op=ALU.add))
                        P.barrier()
                        P.on("vector", lambda e, tkb=tkb: e.tensor_tensor(
                            out=mRb[:], in0=S2buf[:, tkb * 512:(tkb + 1) * 512],
                            in1=mRb[:], op=ALU.max))
                        P.barrier()
                    P.on("gpsimd", lambda e: e.tensor_reduce(mrow[:], mRb[:],
                            axis=AX.C, op=ALU.max))
                    P.barrier()
                    P.on("vector", lambda e: e.tensor_single_scalar(out=mrow[:],
                            in_=mrow[:], scalar=-1.0, op=ALU.mult))
                    P.barrier()
                    P.on("tensor", lambda e: e.matmul(pbc[:], ones_rowb[:],
                            mrow[:], start=True, stop=True))
                    P.barrier()
                    P.on("vector", lambda e: e.tensor_copy(nmB[:], pbc[:]))
                    P.barrier()
                    for tkb in range(8):
                        P.on("vector", lambda e, tkb=tkb: e.tensor_tensor(
                            out=sqt[:, 0:512], in0=S2buf[:, tkb * 512:(tkb + 1) * 512],
                            in1=nmB[:], op=ALU.add))
                        P.barrier()
                        P.on("scalar", lambda e, tkb=tkb: e.activation(
                            out=S2buf[:, tkb * 512:(tkb + 1) * 512],
                            in_=sqt[:, 0:512], func=AF.Exp))
                        P.barrier()
                        P.on("tensor", lambda e, tkb=tkb: e.matmul(
                            pl[:], ones_colb[:], S2buf[:, tkb * 512:(tkb + 1) * 512],
                            start=(tkb == 0), stop=(tkb == 7)))
                        P.on("tensor", lambda e, tkb=tkb: e.matmul(
                            pO[:], VT[:, tkb * 128:(tkb + 1) * 128],
                            S2buf[:, tkb * 512:(tkb + 1) * 512],
                            start=(tkb == 0), stop=(tkb == 7)))
                        P.barrier()
                    P.on("vector", lambda e: e.tensor_copy(lrow[:], pl[:]))
                    P.barrier()
                    P.on("vector", lambda e: e.reciprocal(lrow[:], lrow[:]))
                    P.barrier()
                    P.on("tensor", lambda e: e.matmul(pbc[:], ones_row[:], lrow[:],
                            start=True, stop=True))
                    P.barrier()
                    P.on("vector", lambda e: e.tensor_copy(rlB[:], pbc[:]))
                    P.barrier()
                    P.on("vector", lambda e: e.tensor_tensor(out=ob[:], in0=pO[:],
                            in1=rlB[:], op=ALU.mult))
                    P.barrier()
                    P.dma("gpsimd", lambda e, hh=hh, tqc=tqc: e.dma_start(
                        out=o_d[hh * 128:(hh + 1) * 128, tqc * 512:(tqc + 1) * 512],
                        in_=ob[:]))
                    P.barrier()
            P.on("gpsimd", lambda e: e.engine_nop())
            P.build(nc)
    return nc


def build_f2b():
    nc = bass.Bass("TRN2", target_bir_lowering=False)
    TQ, KB = 256, 16
    xfq_d = nc.dram_tensor("xfq", [2048, TQ], F32, kind="ExternalInput").ap()
    ofq_d = nc.dram_tensor("ofq", [2048, TQ], BF16, kind="ExternalInput").ap()
    rwq_d = nc.dram_tensor("rwq", [1, TQ], F32, kind="ExternalInput").ap()
    g2_d = nc.dram_tensor("g2", [128, 16], F32, kind="ExternalInput").ap()
    wo_d = nc.dram_tensor("wo", [16, 128, 16, 128], F32, kind="ExternalInput").ap()
    w1_d = nc.dram_tensor("w1", [64, 128, 16, 128], F32, kind="ExternalInput").ap()
    w3_d = nc.dram_tensor("w3", [64, 128, 16, 128], F32, kind="ExternalInput").ap()
    w2_d = nc.dram_tensor("w2", [16, 4, 128, 16, 128], F32, kind="ExternalInput").ap()
    xwr_d = nc.dram_tensor("xwr", [TQ, 2048], F32, kind="ExternalOutput").ap()
    P = Prog()
    with contextlib.ExitStack() as stack:
        sb = lambda n, s, dt=F32: stack.enter_context(nc.sbuf_tensor(n, s, dt))
        obf = sb("obf", [128, KB * TQ], BF16)
        xfq = sb("xfq_s", [128, KB * TQ])
        x1 = sb("x1", [128, KB * TQ])
        h2b = sb("h2b", [128, KB * TQ], BF16)
        Gf = sb("Gf", [128, 64 * TQ], BF16)
        g2c = sb("g2c", [128, 16])
        w_st = sb("w_st", [128, 2048])
        w_bf = sb("w_bf", [128, 2048], BF16)
        w_st2 = sb("w_st2", [128, 2048])
        w_bf2 = sb("w_bf2", [128, 2048], BF16)
        sgb = sb("sgb", [128, TQ], BF16)
        sq = sb("sq", [128, TQ])
        rrow = sb("rrow", [1, TQ])
        rstdB = sb("rstdB", [128, TQ])
        rwB = sb("rwB", [128, TQ])
        xw = sb("xw", [128, TQ])
        otile = sb("otile", [128, 128])
        identf = sb("identf", [128, 128])
        ones_col = sb("ones_col", [128, 1])
        ones_row = sb("ones_row", [1, 128])
        epsc = sb("epsc", [1, 1])
        with contextlib.ExitStack() as pstack:
            pp = lambda n, s, dt=F32: pstack.enter_context(nc.psum_tensor(n, s, dt))
            pg = pp("pg", [128, 256])
            pu = pp("pu", [128, 256])
            pa = pp("pa", [128, 256])
            pr = pp("pr", [1, 256])
            pbc = pp("pbc", [128, 256])
            pTf = pp("pTf", [128, 128])
            P.on("vector", lambda e: e.memset(ones_col[:], 1.0))
            P.on("vector", lambda e: e.memset(ones_row[:], 1.0))
            P.on("vector", lambda e: e.memset(epsc[:], 1e-6))
            P.dma("sync", lambda e: e.dma_start(out=g2c[:], in_=g2_d[:, :]))
            P.dma("sync", lambda e: e.dma_start(out=rrow[:], in_=rwq_d[:, :]))
            for mb in range(16):
                P.dma("sync", lambda e, mb=mb: e.dma_start(
                    out=obf[:, mb * TQ:(mb + 1) * TQ],
                    in_=ofq_d[mb * 128:(mb + 1) * 128, :]))
                P.dma("sync", lambda e, mb=mb: e.dma_start(
                    out=xfq[:, mb * TQ:(mb + 1) * TQ],
                    in_=xfq_d[mb * 128:(mb + 1) * 128, :]))
            P.barrier()
            _mk_ident(P, identf)
            P.on("tensor", lambda e: e.matmul(pbc[:], ones_row[:], rrow[:],
                    start=True, stop=True))
            P.barrier()
            P.on("vector", lambda e: e.tensor_copy(rwB[:], pbc[:]))
            P.barrier()
            wcnt = []
            def issue_w(dram, mb, stream="ww"):
                P.qwait("sync")
                return P.sdma("sync", stream, lambda e, dram=dram, mb=mb: e.dma_start(
                    out=w_st[:], in_=dram[mb]))
            wcnt.append(issue_w(wo_d, 0))
            for mb in range(16):
                P.wait_stream("scalar", "ww", wcnt[mb])
                P.on("scalar", lambda e: e.activation(out=w_bf[:], in_=w_st[:],
                                                      func=AF.Copy))
                P.barrier()
                if mb + 1 < 16:
                    wcnt.append(issue_w(wo_d, mb + 1))
                for kb in range(KB):
                    P.on("tensor", lambda e, kb=kb: e.matmul(pa[:],
                            w_bf[:, kb * 128:(kb + 1) * 128],
                            obf[:, kb * TQ:(kb + 1) * TQ],
                            start=(kb == 0), stop=(kb == KB - 1)))
                P.barrier()
                P.on("vector", lambda e, mb=mb: e.tensor_tensor(
                    out=x1[:, mb * TQ:(mb + 1) * TQ], in0=pa[:],
                    in1=xfq[:, mb * TQ:(mb + 1) * TQ], op=ALU.add))
                P.barrier()
            for mb in range(16):
                P.on("scalar", lambda e, mb=mb: e.activation(out=sq[:],
                        in_=x1[:, mb * TQ:(mb + 1) * TQ], func=AF.Square))
                P.barrier()
                P.on("tensor", lambda e, mb=mb: e.matmul(pr[:], ones_col[:], sq[:],
                        start=(mb == 0), stop=(mb == 15)))
                P.barrier()
            P.on("vector", lambda e: e.tensor_copy(rrow[:], pr[:]))
            P.barrier()
            P.on("scalar", lambda e: e.activation(out=rrow[:], in_=rrow[:],
                    func=AF.Copy, scale=1.0 / 2048.0))
            P.barrier()
            P.on("scalar", lambda e: e.activation(out=rrow[:], in_=rrow[:],
                    func=AF.Sqrt, bias=epsc[:]))
            P.barrier()
            P.on("vector", lambda e: e.reciprocal(rrow[:], rrow[:]))
            P.barrier()
            P.on("tensor", lambda e: e.matmul(pbc[:], ones_row[:], rrow[:],
                    start=True, stop=True))
            P.barrier()
            P.on("vector", lambda e: e.tensor_copy(rstdB[:], pbc[:]))
            P.barrier()
            for mb in range(16):
                P.on("vector", lambda e, mb=mb: e.tensor_tensor(out=sq[:],
                        in0=x1[:, mb * TQ:(mb + 1) * TQ], in1=rstdB[:], op=ALU.mult))
                P.barrier()
                P.on("vector", lambda e, mb=mb: e.tensor_scalar(
                    out=h2b[:, mb * TQ:(mb + 1) * TQ], in0=sq[:],
                    scalar1=g2c[:, mb:mb + 1], scalar2=None, op0=ALU.mult))
                P.barrier()
            c1 = []
            c3 = []
            def issue_13(mb):
                P.qwait("sync")
                c1.append(P.sdma("sync", "w1", lambda e, mb=mb: e.dma_start(
                    out=w_st[:], in_=w1_d[mb])))
                P.qwait("gpsimd")
                c3.append(P.sdma("gpsimd", "w3", lambda e, mb=mb: e.dma_start(
                    out=w_st2[:], in_=w3_d[mb])))
            issue_13(0)
            for mb in range(64):
                P.wait_stream("scalar", "w1", c1[mb])
                P.on("scalar", lambda e: e.activation(out=w_bf[:], in_=w_st[:],
                                                      func=AF.Copy))
                P.wait_stream("vector", "w3", c3[mb])
                P.on("vector", lambda e: e.tensor_copy(w_bf2[:], w_st2[:]))
                P.barrier()
                if mb + 1 < 64:
                    issue_13(mb + 1)
                for kb in range(KB):
                    P.on("tensor", lambda e, kb=kb: e.matmul(pg[:],
                            w_bf[:, kb * 128:(kb + 1) * 128],
                            h2b[:, kb * TQ:(kb + 1) * TQ],
                            start=(kb == 0), stop=(kb == KB - 1)))
                for kb in range(KB):
                    P.on("tensor", lambda e, kb=kb: e.matmul(pu[:],
                            w_bf2[:, kb * 128:(kb + 1) * 128],
                            h2b[:, kb * TQ:(kb + 1) * TQ],
                            start=(kb == 0), stop=(kb == KB - 1)))
                P.barrier()
                P.on("scalar", lambda e: e.activation(out=sgb[:], in_=pg[:],
                                                      func=AF.Silu))
                P.barrier()
                P.on("vector", lambda e, mb=mb: e.tensor_tensor(
                    out=Gf[:, mb * TQ:(mb + 1) * TQ], in0=pu[:], in1=sgb[:],
                    op=ALU.mult))
                P.barrier()
            c2 = []
            def issue_2(mb2, kq):
                P.qwait("sync")
                c2.append(P.sdma("sync", "w2", lambda e, mb2=mb2, kq=kq: e.dma_start(
                    out=w_st[:], in_=w2_d[mb2, kq])))
            issue_2(0, 0)
            w2i = 0
            for mb2 in range(16):
                for kq in range(4):
                    P.wait_stream("scalar", "w2", c2[w2i])
                    P.on("scalar", lambda e: e.activation(out=w_bf[:], in_=w_st[:],
                                                          func=AF.Copy))
                    P.barrier()
                    w2i += 1
                    if kq + 1 < 4:
                        issue_2(mb2, kq + 1)
                    elif mb2 + 1 < 16:
                        issue_2(mb2 + 1, 0)
                    for kb in range(KB):
                        gi = kq * 16 + kb
                        P.on("tensor", lambda e, kb=kb, gi=gi, kq=kq: e.matmul(pa[:],
                                w_bf[:, kb * 128:(kb + 1) * 128],
                                Gf[:, gi * TQ:(gi + 1) * TQ],
                                start=(gi == 0), stop=(gi == 63)))
                    P.barrier()
                P.on("vector", lambda e, mb2=mb2: e.tensor_tensor(out=xw[:],
                        in0=pa[:], in1=x1[:, mb2 * TQ:(mb2 + 1) * TQ], op=ALU.add))
                P.barrier()
                P.on("vector", lambda e: e.tensor_tensor(out=xw[:], in0=xw[:],
                        in1=rwB[:], op=ALU.mult))
                P.barrier()
                for tb in range(2):
                    P.on("tensor", lambda e, tb=tb: e.transpose(pTf[:],
                            xw[:, tb * 128:(tb + 1) * 128], identf[:]))
                    P.barrier()
                    P.on("vector", lambda e: e.tensor_copy(otile[:], pTf[:]))
                    P.barrier()
                    P.dma("gpsimd", lambda e, mb2=mb2, tb=tb: e.dma_start(
                        out=xwr_d[tb * 128:(tb + 1) * 128, mb2 * 128:(mb2 + 1) * 128],
                        in_=otile[:]))
                    P.barrier()
        P.on("gpsimd", lambda e: e.engine_nop())
        P.build(nc)
    return nc


def build_f3():
    nc = bass.Bass("TRN2", target_bir_lowering=False)
    xr_d = nc.dram_tensor("xr", [1024, 2048], F32, kind="ExternalInput").ap()
    xlr_d = nc.dram_tensor("xlr", [1024, 2048], F32, kind="ExternalInput").ap()
    xwr_d = nc.dram_tensor("xwr", [1024, 2048], F32, kind="ExternalInput").ap()
    sel_d = nc.dram_tensor("sel", [1024, 1], F32, kind="ExternalInput").ap()
    r0_d = nc.dram_tensor("r0", [128, 1], F32, kind="ExternalInput").ap()
    out_d = nc.dram_tensor("out", [1024, 2048], F32, kind="ExternalOutput").ap()
    tacc_d = nc.dram_tensor("tacc", [1024, 2048], F32, kind="Internal").ap()
    P = Prog()
    with contextlib.ExitStack() as stack:
        sb = lambda n, s, dt=F32: stack.enter_context(nc.sbuf_tensor(n, s, dt))
        selc = sb("selc", [128, 8])
        r0c = sb("r0c", [128, 1])
        offs = sb("offs", [128, 8])
        offi = sb("offi", [128, 8], I32)
        msk = sb("msk", [128, 8])
        ta = sb("ta", [128, 2048])
        xb = sb("xb", [128, 2048])
        xwsb = sb("xwsb", [128, 2048])
        P.dma("sync", lambda e: e.dma_start(out=selc[:],
                in_=sel_d.rearrange("(c p) o -> (o p) c", p=128),
                allow_slow_non_contiguous=True))
        P.dma("sync", lambda e: e.dma_start(out=r0c[:], in_=r0_d[:, :]))
        P.barrier()
        P.on("vector", lambda e: e.tensor_scalar(out=offs[:], in0=selc[:],
                scalar1=r0c[:], scalar2=None, op0=ALU.subtract))
        P.barrier()
        P.on("vector", lambda e: e.tensor_single_scalar(out=msk[:], in_=offs[:],
                scalar=0.0, op=ALU.is_lt))
        P.barrier()
        P.on("vector", lambda e: e.scalar_tensor_tensor(out=offs[:], in0=msk[:],
                scalar=3000000.0, in1=offs[:], op0=ALU.mult, op1=ALU.add))
        P.barrier()
        P.on("vector", lambda e: e.tensor_copy(offi[:], offs[:]))
        P.barrier()
        for rb in range(8):
            P.dma("sync", lambda e, rb=rb: e.dma_start(out=ta[:],
                    in_=xlr_d[rb * 128:(rb + 1) * 128, :]))
            P.barrier()
            P.dma("sync", lambda e, rb=rb: e.dma_start(
                out=tacc_d[rb * 128:(rb + 1) * 128, :], in_=ta[:]))
            P.barrier()
        for ch in range(8):
            P.dma("gpsimd", lambda e, ch=ch: e.dma_start(out=xwsb[:],
                    in_=xwr_d[ch * 128:(ch + 1) * 128, :]))
            P.barrier()
            P.dma("gpsimd", lambda e, ch=ch: e.indirect_dma_start(
                out=tacc_d[:, :], out_offset=IndirectOffsetOnAxis(offi[:, ch:ch + 1], 0),
                in_=xwsb[:], in_offset=None,
                bounds_check=1023, oob_is_err=False, compute_op=ALU.add))
            P.barrier()
        for rb in range(8):
            P.dma("sync", lambda e, rb=rb: e.dma_start(out=ta[:],
                    in_=tacc_d[rb * 128:(rb + 1) * 128, :]))
            P.dma("sync", lambda e, rb=rb: e.dma_start(out=xb[:],
                    in_=xr_d[rb * 128:(rb + 1) * 128, :]))
            P.barrier()
            P.on("vector", lambda e: e.scalar_tensor_tensor(out=ta[:], in0=ta[:],
                    scalar=0.5, in1=xb[:], op0=ALU.mult, op1=ALU.add))
            P.barrier()
            P.dma("gpsimd", lambda e, rb=rb: e.dma_start(
                out=out_d[rb * 128:(rb + 1) * 128, :], in_=ta[:]))
            P.barrier()
        P.on("gpsimd", lambda e: e.engine_nop())
        P.build(nc)
    return nc



def build_a1():
    nc = bass.Bass("TRN2", target_bir_lowering=False)
    TB, KB = 4096, 16
    xt_d = nc.dram_tensor("xt", [2048, TB], F32, kind="ExternalInput").ap()
    wq_d = nc.dram_tensor("wq", [4, 128, 16, 128], F32, kind="ExternalInput").ap()
    wk_d = nc.dram_tensor("wk", [1, 128, 16, 128], F32, kind="ExternalInput").ap()
    wv_d = nc.dram_tensor("wv", [1, 128, 16, 128], F32, kind="ExternalInput").ap()
    qg_d = nc.dram_tensor("qg", [128, 1], F32, kind="ExternalInput").ap()
    kg_d = nc.dram_tensor("kg", [128, 1], F32, kind="ExternalInput").ap()
    o_d = nc.dram_tensor("o", [512, TB], BF16, kind="ExternalOutput").ap()
    P = Prog()
    with contextlib.ExitStack() as stack:
        sb = lambda n, s, dt=F32: stack.enter_context(nc.sbuf_tensor(n, s, dt))
        xst = sb("xst", [128, 1024])
        sqt = sb("sqt", [128, 1024])
        statA = sb("statA", [1, 512])
        statB = sb("statB", [1, 512])
        rstdB = sb("rstdB", [128, 512])
        mrsB = sb("mrsB", [128, 512])
        xlnc = sb("xlnc", [128, KB * 512], BF16)
        Qf = sb("Qf", [128, 4 * TB], BF16)
        Kf = sb("Kf", [128, TB], BF16)
        Vf = sb("Vf", [128, TB], BF16)
        VT = sb("VT", [128, 32 * 128], BF16)
        sqbig = sb("sqbig", [128, 4096])
        rqrow = sb("rqrow", [1, 4096])
        w_st = sb("w_st", [128, 2048])
        wqb = sb("wqb", [128, 4 * 16 * 128], BF16)
        wkb = sb("wkb", [128, 2048], BF16)
        wvb = sb("wvb", [128, 2048], BF16)
        qgc = sb("qgc", [128, 1])
        kgc = sb("kgc", [128, 1])
        amask = sb("amask", [128, 4 * 512])
        S2buf = sb("S2buf", [128, 32 * 512], BF16)
        mRb = sb("mRb", [128, 512], BF16)
        mrow = sb("mrow", [1, 512], BF16)
        nmB = sb("nmB", [128, 512], BF16)
        lrow = sb("lrow", [1, 512])
        rlB = sb("rlB", [128, 512])
        ob = sb("ob", [128, 512], BF16)
        rbb = sb("rbb", [128, 512], BF16)
        identb = sb("identb", [128, 128], BF16)
        ones_col = sb("ones_col", [128, 1])
        ones_colb = sb("ones_colb", [128, 1], BF16)
        ones_row = sb("ones_row", [1, 128])
        ones_rowb = sb("ones_rowb", [1, 128], BF16)
        epsc = sb("epsc", [1, 1])
        epsq = sb("epsq", [1, 1])
        with contextlib.ExitStack() as pstack:
            pp = lambda n, s, dt=F32: pstack.enter_context(nc.psum_tensor(n, s, dt))
            pbank = [pp(f"p{i}", [128, 512]) for i in range(7)]
            pTb = pp("pTb", [128, 128], BF16)
            P.on("vector", lambda e: e.memset(ones_col[:], 1.0))
            P.on("vector", lambda e: e.memset(ones_colb[:], 1.0))
            P.on("vector", lambda e: e.memset(ones_row[:], 1.0))
            P.on("vector", lambda e: e.memset(ones_rowb[:], 1.0))
            P.on("vector", lambda e: e.memset(epsc[:], 1e-5))
            P.on("vector", lambda e: e.memset(epsq[:], 1e-8))
            P.on("gpsimd", lambda e: e.memset(amask[:], 0.0))
            P.dma("sync", lambda e: e.dma_start(out=qgc[:], in_=qg_d[:, :]))
            P.dma("sync", lambda e: e.dma_start(out=kgc[:], in_=kg_d[:, :]))
            P.barrier()
            _mk_ident(P, identb)
            for r in range(4):
                P.on("gpsimd", lambda e, r=r: e.affine_select(
                    out=amask[:, r * 512:(r + 1) * 512],
                    in_=amask[:, r * 512:(r + 1) * 512],
                    compare_op=ALU.is_ge, fill=-1e30, base=-128 * r,
                    pattern=[[1, 512]], channel_multiplier=-1))
            P.on("scalar", lambda e: e.activation(out=qgc[:], in_=qgc[:], func=AF.Copy,
                    scale=0.08838834764831845))
            P.barrier()
            for name, wd, dst, nmb in (("q", wq_d, wqb, 4), ("k", wk_d, wkb, 1),
                                       ("v", wv_d, wvb, 1)):
                for hh in range(nmb):
                    cw = P.sdma("sync", "ww", lambda e, wd=wd, hh=hh: e.dma_start(
                        out=w_st[:, 0:2048], in_=wd[hh]))
                    P.wait_stream("scalar", "ww", cw)
                    P.on("scalar", lambda e, dst=dst, hh=hh: e.activation(
                        out=dst[:, hh * 2048:(hh + 1) * 2048], in_=w_st[:, 0:2048],
                        func=AF.Copy))
                    P.barrier()
                    P.qwait("sync")
            for tc in range(8):
                cnts = []
                P.qwait("sync")
                for kb in range(2):
                    cnts.append(P.sdma("sync", "wx", lambda e, kb=kb, tc=tc:
                        e.dma_start(out=xst[:, (kb % 2) * 512:(kb % 2) * 512 + 512],
                                    in_=xt_d[kb * 128:(kb + 1) * 128,
                                             tc * 512:(tc + 1) * 512])))
                for kb in range(16):
                    slot = kb % 2
                    P.wait_stream("scalar", "wx", cnts[kb])
                    P.on("scalar", lambda e, slot=slot: e.activation(
                        out=sqt[:, slot * 512:slot * 512 + 512],
                        in_=xst[:, slot * 512:slot * 512 + 512], func=AF.Square))
                    P.barrier()
                    P.on("tensor", lambda e, slot=slot, kb=kb: e.matmul(
                        pbank[5][0:1, :], ones_col[:],
                        xst[:, slot * 512:slot * 512 + 512],
                        start=(kb == 0), stop=(kb == 15)))
                    P.on("tensor", lambda e, slot=slot, kb=kb: e.matmul(
                        pbank[6][0:1, :], ones_col[:],
                        sqt[:, slot * 512:slot * 512 + 512],
                        start=(kb == 0), stop=(kb == 15)))
                    P.barrier()
                    if kb + 2 < 16:
                        P.qwait("sync")
                        cnts.append(P.sdma("sync", "wx", lambda e, kb=kb, tc=tc,
                            slot=slot: e.dma_start(
                                out=xst[:, slot * 512:slot * 512 + 512],
                                in_=xt_d[(kb + 2) * 128:(kb + 3) * 128,
                                         tc * 512:(tc + 1) * 512])))
                mean = rstdB[0:1, 0:512]
                ms = mrsB[0:1, 0:512]
                P.on("vector", lambda e: e.tensor_copy(mean[:], pbank[5][0:1, :]))
                P.barrier()
                P.on("scalar", lambda e: e.activation(out=mean[:], in_=mean[:],
                        func=AF.Copy, scale=1.0 / 2048.0))
                P.on("scalar", lambda e: e.activation(out=ms[:], in_=pbank[6][0:1, :],
                        func=AF.Copy, scale=1.0 / 2048.0))
                P.barrier()
                P.on("scalar", lambda e: e.activation(out=statA[:], in_=mean[:],
                        func=AF.Square))
                P.barrier()
                P.on("vector", lambda e: e.tensor_tensor(out=statA[:], in0=ms[:],
                        in1=statA[:], op=ALU.subtract))
                P.barrier()
                P.on("scalar", lambda e: e.activation(out=statB[:], in_=statA[:],
                        func=AF.Sqrt, bias=epsc[:]))
                P.barrier()
                P.on("vector", lambda e: e.reciprocal(statA[:], statB[:]))
                P.barrier()
                P.on("vector", lambda e: e.tensor_tensor(out=statB[:], in0=mean[:],
                        in1=statA[:], op=ALU.mult))
                P.barrier()
                P.on("tensor", lambda e: e.matmul(pbank[5][:], ones_row[:], statA[:],
                        start=True, stop=True))
                P.on("tensor", lambda e: e.matmul(pbank[6][:], ones_row[:], statB[:],
                        start=True, stop=True))
                P.barrier()
                P.on("vector", lambda e: e.tensor_copy(rstdB[:], pbank[5][:]))
                P.on("vector", lambda e: e.tensor_copy(mrsB[:], pbank[6][:]))
                P.barrier()
                P.qwait("sync")
                cnts2 = []
                for kb in range(2):
                    cnts2.append(P.sdma("sync", "wx", lambda e, kb=kb, tc=tc:
                        e.dma_start(out=xst[:, (kb % 2) * 512:(kb % 2) * 512 + 512],
                                    in_=xt_d[kb * 128:(kb + 1) * 128,
                                             tc * 512:(tc + 1) * 512])))
                for kb in range(16):
                    slot = kb % 2
                    P.wait_stream("vector", "wx", cnts2[kb])
                    P.on("vector", lambda e, slot=slot: e.tensor_tensor(
                        out=sqt[:, slot * 512:slot * 512 + 512],
                        in0=xst[:, slot * 512:slot * 512 + 512], in1=rstdB[:],
                        op=ALU.mult))
                    P.barrier()
                    P.on("vector", lambda e, slot=slot, kb=kb: e.tensor_tensor(
                        out=xlnc[:, kb * 512:(kb + 1) * 512],
                        in0=sqt[:, slot * 512:slot * 512 + 512], in1=mrsB[:],
                        op=ALU.subtract))
                    P.barrier()
                    if kb + 2 < 16:
                        P.qwait("sync")
                        cnts2.append(P.sdma("sync", "wx", lambda e, kb=kb, tc=tc,
                            slot=slot: e.dma_start(
                                out=xst[:, slot * 512:slot * 512 + 512],
                                in_=xt_d[(kb + 2) * 128:(kb + 3) * 128,
                                         tc * 512:(tc + 1) * 512])))
                for mi in range(6):
                    if mi < 4:
                        lhs = lambda kb, mi=mi: wqb[:, mi * 2048 + kb * 128:
                                                    mi * 2048 + (kb + 1) * 128]
                    elif mi == 4:
                        lhs = lambda kb: wkb[:, kb * 128:(kb + 1) * 128]
                    else:
                        lhs = lambda kb: wvb[:, kb * 128:(kb + 1) * 128]
                    for kb in range(16):
                        P.on("tensor", lambda e, mi=mi, kb=kb, lhs=lhs: e.matmul(
                            pbank[mi][:], lhs(kb), xlnc[:, kb * 512:(kb + 1) * 512],
                            start=(kb == 0), stop=(kb == 15)))
                P.barrier()
                for mi in range(4):
                    P.on("vector", lambda e, mi=mi, tc=tc: e.tensor_copy(
                        Qf[:, mi * TB + tc * 512:mi * TB + tc * 512 + 512],
                        pbank[mi][:]))
                P.on("vector", lambda e, tc=tc: e.tensor_copy(
                    Kf[:, tc * 512:tc * 512 + 512], pbank[4][:]))
                P.on("scalar", lambda e, tc=tc: e.activation(
                    out=Vf[:, tc * 512:tc * 512 + 512], in_=pbank[5][:], func=AF.Copy))
                P.barrier()
            for t, gcol, nh in ((Qf, qgc, 4), (Kf, kgc, 1)):
                for hh in range(nh):
                    base = hh * TB
                    P.on("scalar", lambda e, t=t, base=base: e.activation(
                        out=sqbig[:], in_=t[:, base:base + TB], func=AF.Square))
                    P.barrier()
                    for c8 in range(8):
                        P.on("tensor", lambda e, c8=c8: e.matmul(
                            pbank[6][0:1, :], ones_col[:],
                            sqbig[:, c8 * 512:(c8 + 1) * 512], start=True, stop=True))
                        P.barrier()
                        P.on("vector", lambda e, c8=c8: e.tensor_copy(
                            rqrow[:, c8 * 512:c8 * 512 + 512], pbank[6][0:1, :]))
                        P.barrier()
                    P.on("scalar", lambda e: e.activation(out=rqrow[:], in_=rqrow[:],
                            func=AF.Copy, scale=1.0 / 128.0))
                    P.barrier()
                    P.on("scalar", lambda e: e.activation(out=rqrow[:], in_=rqrow[:],
                            func=AF.Sqrt, bias=epsq[:]))
                    P.barrier()
                    P.on("vector", lambda e: e.reciprocal(rqrow[:], rqrow[:]))
                    P.barrier()
                    for c8 in range(8):
                        P.on("tensor", lambda e, c8=c8: e.matmul(pbank[6][:],
                                ones_row[:], rqrow[:, c8 * 512:c8 * 512 + 512],
                                start=True, stop=True))
                        P.barrier()
                        P.on("vector", lambda e: e.tensor_copy(rbb[:], pbank[6][:]))
                        P.barrier()
                        P.on("vector", lambda e, t=t, base=base, c8=c8, gcol=gcol:
                            e.tensor_scalar(
                                out=sqbig[:, 0:512],
                                in0=t[:, base + c8 * 512:base + c8 * 512 + 512],
                                scalar1=gcol[:], scalar2=None, op0=ALU.mult))
                        P.barrier()
                        P.on("vector", lambda e, t=t, base=base, c8=c8: e.tensor_tensor(
                            out=t[:, base + c8 * 512:base + c8 * 512 + 512],
                            in0=sqbig[:, 0:512], in1=rbb[:], op=ALU.mult))
                        P.barrier()
            for tkb in range(32):
                P.on("tensor", lambda e, tkb=tkb: e.transpose(
                    pTb[:], Vf[:, tkb * 128:(tkb + 1) * 128], identb[:]))
                P.barrier()
                P.on("vector", lambda e, tkb=tkb: e.tensor_copy(
                    VT[:, tkb * 128:(tkb + 1) * 128], pTb[:]))
                P.barrier()
            for qh in range(4):
                for tqc in range(8):
                    nblk = 4 * tqc + 4
                    P.on("vector", lambda e: e.memset(mRb[:], -1e38))
                    P.barrier()
                    for tkb in range(nblk):
                        r = tkb - 4 * tqc
                        P.on("tensor", lambda e, tkb=tkb, qh=qh, tqc=tqc: e.matmul(
                            pbank[0][:], Kf[:, tkb * 128:(tkb + 1) * 128],
                            Qf[:, qh * TB + tqc * 512:qh * TB + tqc * 512 + 512],
                            start=True, stop=True))
                        P.barrier()
                        if r >= 0:
                            P.on("vector", lambda e, tkb=tkb, r=r: e.tensor_tensor(
                                out=S2buf[:, tkb * 512:(tkb + 1) * 512], in0=pbank[0][:],
                                in1=amask[:, r * 512:(r + 1) * 512], op=ALU.add))
                        else:
                            P.on("vector", lambda e, tkb=tkb: e.tensor_copy(
                                S2buf[:, tkb * 512:(tkb + 1) * 512], pbank[0][:]))
                        P.barrier()
                        P.on("vector", lambda e, tkb=tkb: e.tensor_tensor(
                            out=mRb[:], in0=S2buf[:, tkb * 512:(tkb + 1) * 512],
                            in1=mRb[:], op=ALU.max))
                        P.barrier()
                    P.on("gpsimd", lambda e: e.tensor_reduce(mrow[:], mRb[:],
                            axis=AX.C, op=ALU.max))
                    P.barrier()
                    P.on("vector", lambda e: e.tensor_single_scalar(out=mrow[:],
                            in_=mrow[:], scalar=-1.0, op=ALU.mult))
                    P.barrier()
                    P.on("tensor", lambda e: e.matmul(pbank[1][:], ones_rowb[:],
                            mrow[:], start=True, stop=True))
                    P.barrier()
                    P.on("vector", lambda e: e.tensor_copy(nmB[:], pbank[1][:]))
                    P.barrier()
                    for tkb in range(nblk):
                        P.on("vector", lambda e, tkb=tkb: e.tensor_tensor(
                            out=sqt[:, 0:512], in0=S2buf[:, tkb * 512:(tkb + 1) * 512],
                            in1=nmB[:], op=ALU.add))
                        P.barrier()
                        P.on("scalar", lambda e, tkb=tkb: e.activation(
                            out=S2buf[:, tkb * 512:(tkb + 1) * 512],
                            in_=sqt[:, 0:512], func=AF.Exp))
                        P.barrier()
                        P.on("tensor", lambda e, tkb=tkb, nblk=nblk: e.matmul(
                            pbank[2][0:1, :], ones_colb[:],
                            S2buf[:, tkb * 512:(tkb + 1) * 512],
                            start=(tkb == 0), stop=(tkb == nblk - 1)))
                        P.on("tensor", lambda e, tkb=tkb, nblk=nblk: e.matmul(
                            pbank[3][:], VT[:, tkb * 128:(tkb + 1) * 128],
                            S2buf[:, tkb * 512:(tkb + 1) * 512],
                            start=(tkb == 0), stop=(tkb == nblk - 1)))
                        P.barrier()
                    P.on("vector", lambda e: e.tensor_copy(lrow[:], pbank[2][0:1, :]))
                    P.barrier()
                    P.on("vector", lambda e: e.reciprocal(lrow[:], lrow[:]))
                    P.barrier()
                    P.on("tensor", lambda e: e.matmul(pbank[1][:], ones_row[:],
                            lrow[:], start=True, stop=True))
                    P.barrier()
                    P.on("vector", lambda e: e.tensor_copy(rlB[:], pbank[1][:]))
                    P.barrier()
                    P.on("vector", lambda e: e.tensor_tensor(out=ob[:],
                            in0=pbank[3][:], in1=rlB[:], op=ALU.mult))
                    P.barrier()
                    P.dma("gpsimd", lambda e, qh=qh, tqc=tqc: e.dma_start(
                        out=o_d[qh * 128:(qh + 1) * 128, tqc * 512:(tqc + 1) * 512],
                        in_=ob[:]))
                    P.barrier()
            P.on("gpsimd", lambda e: e.engine_nop())
            P.build(nc)
    return nc


def build_a2f1():
    nc = bass.Bass("TRN2", target_bir_lowering=False)
    T, KB = 1024, 16
    xt_d = nc.dram_tensor("xt", [2048, T], F32, kind="ExternalInput").ap()
    of_d = nc.dram_tensor("of", [2048, T], BF16, kind="ExternalInput").ap()
    wo_d = nc.dram_tensor("wo", [16, 128, 16, 128], F32, kind="ExternalInput").ap()
    wr_d = nc.dram_tensor("wr", [128, 16], F32, kind="ExternalInput").ap()
    out_d = nc.dram_tensor("out", [2048, T], F32, kind="ExternalOutput").ap()
    xlnr_d = nc.dram_tensor("xlnr", [T, 2048], F32, kind="ExternalOutput").ap()
    lgr_d = nc.dram_tensor("lgr", [1, T], F32, kind="ExternalOutput").ap()
    x5d_d = nc.dram_tensor("x5d", [2048, T], F32, kind="Internal").ap()
    P = Prog()
    with contextlib.ExitStack() as stack:
        sb = lambda n, s, dt=F32: stack.enter_context(nc.sbuf_tensor(n, s, dt))
        xst = sb("xst", [128, 1024])
        sqt = sb("sqt", [128, 1024])
        statA = sb("statA", [1, T])
        statB = sb("statB", [1, T])
        rstdB = sb("rstdB", [128, T])
        mrsB = sb("mrsB", [128, T])
        xln = sb("xln", [128, T], BF16)
        xlf = sb("xlf", [128, T])
        obf = sb("obf", [128, KB * T], BF16)
        w_st = sb("w_st", [128, 2048])
        w_bf = sb("w_bf", [128, 2048], BF16)
        wrt = sb("wrt", [128, 16])
        lrow = sb("lrow", [1, T])
        otile = sb("otile", [128, 128])
        x5t = sb("x5t", [128, 1024])
        identf = sb("identf", [128, 128])
        ones_col = sb("ones_col", [128, 1])
        ones_row = sb("ones_row", [1, 128])
        epsc = sb("epsc", [1, 1])
        pbank = [stack.enter_context(nc.psum_tensor(f"p{i}", [128, 512], F32))
                 for i in range(8)]
        P.on("vector", lambda e: e.memset(ones_col[:], 1.0))
        P.on("vector", lambda e: e.memset(ones_row[:], 1.0))
        P.on("vector", lambda e: e.memset(epsc[:], 1e-5))
        P.dma("sync", lambda e: e.dma_start(out=wrt[:], in_=wr_d[:, :]))
        for mb in range(16):
            P.dma("sync", lambda e, mb=mb: e.dma_start(
                out=obf[:, mb * T:(mb + 1) * T], in_=of_d[mb * 128:(mb + 1) * 128, :]))
        P.barrier()
        _mk_ident(P, identf)
        emit_ln_fm(P, nc, xt_d, xst, sqt, statA, statB, rstdB, mrsB, xln, ones_col,
                   ones_row, pbank, epsc, T=T, KB=KB, apply=False)
        wcnt = []
        def issue_w(mb):
            P.qwait("sync")
            wcnt.append(P.sdma("sync", "ww", lambda e, mb=mb: e.dma_start(
                out=w_st[:], in_=wo_d[mb])))
        issue_w(0)
        xcnt = []
        P.qwait("gpsimd")
        xcnt.append(P.sdma("gpsimd", "wx2", lambda e: e.dma_start(
            out=xst[:], in_=xt_d[0:128, :])))
        for mb in range(16):
            P.wait_stream("scalar", "ww", wcnt[mb])
            P.on("scalar", lambda e: e.activation(out=w_bf[:], in_=w_st[:],
                                                  func=AF.Copy))
            P.barrier()
            if mb + 1 < 16:
                issue_w(mb + 1)
            for h in range(2):
                for kb in range(KB):
                    P.on("tensor", lambda e, h=h, kb=kb: e.matmul(
                        pbank[h][:], w_bf[:, kb * 128:(kb + 1) * 128],
                        obf[:, kb * T + h * 512:kb * T + h * 512 + 512],
                        start=(kb == 0), stop=(kb == KB - 1)))
            P.wait_stream("vector", "wx2", xcnt[mb])
            P.barrier()
            P.on("vector", lambda e: e.tensor_tensor(out=sqt[:], in0=xst[:],
                    in1=rstdB[:], op=ALU.mult))
            P.barrier()
            P.on("vector", lambda e: e.tensor_tensor(out=sqt[:], in0=sqt[:],
                    in1=mrsB[:], op=ALU.subtract))
            P.barrier()
            P.on("vector", lambda e: e.tensor_tensor(out=sqt[:], in0=sqt[:],
                    in1=xst[:], op=ALU.add))
            P.barrier()
            if mb + 1 < 16:
                P.qwait("gpsimd")
                xcnt.append(P.sdma("gpsimd", "wx2", lambda e, mb=mb: e.dma_start(
                    out=xst[:], in_=xt_d[(mb + 1) * 128:(mb + 2) * 128, :])))
            for h in range(2):
                P.on("vector", lambda e, h=h: e.tensor_tensor(
                    out=x5t[:, h * 512:h * 512 + 512], in0=pbank[h][:],
                    in1=sqt[:, h * 512:h * 512 + 512], op=ALU.add))
            P.barrier()
            P.dma("gpsimd", lambda e, mb=mb: e.dma_start(
                out=out_d[mb * 128:(mb + 1) * 128, :], in_=x5t[:]))
            P.dma("gpsimd", lambda e, mb=mb: e.dma_start(
                out=x5d_d[mb * 128:(mb + 1) * 128, :], in_=x5t[:]))
            P.barrier()
        emit_ln_fm(P, nc, x5d_d, xst, sqt, statA, statB, rstdB, mrsB, xln, ones_col,
                   ones_row, pbank, epsc, T=T, KB=KB, apply=False)
        P.qwait("sync")
        cnts = [P.sdma("sync", "wx", lambda e: e.dma_start(
            out=xst[:], in_=x5d_d[0:128, :]))]
        for kb in range(KB):
            P.wait_stream("vector", "wx", cnts[kb])
            P.on("vector", lambda e: e.tensor_tensor(out=sqt[:], in0=xst[:],
                    in1=rstdB[:], op=ALU.mult))
            P.barrier()
            if kb + 1 < KB:
                P.qwait("sync")
                cnts.append(P.sdma("sync", "wx", lambda e, kb=kb: e.dma_start(
                    out=xst[:], in_=x5d_d[(kb + 1) * 128:(kb + 2) * 128, :])))
            P.on("vector", lambda e: e.tensor_tensor(out=xlf[:], in0=sqt[:],
                    in1=mrsB[:], op=ALU.subtract))
            P.barrier()
            for h in range(2):
                P.on("tensor", lambda e, kb=kb, h=h: e.matmul(
                    pbank[4 + h][0:1, :], wrt[:, kb:kb + 1],
                    xlf[:, h * 512:h * 512 + 512],
                    start=(kb == 0), stop=(kb == KB - 1)))
            P.barrier()
            for tb in range(8):
                P.on("tensor", lambda e, tb=tb: e.transpose(
                    pbank[6][:, 0:128], xlf[:, tb * 128:(tb + 1) * 128], identf[:]))
                P.barrier()
                P.on("vector", lambda e: e.tensor_copy(otile[:], pbank[6][:, 0:128]))
                P.barrier()
                P.dma("gpsimd", lambda e, tb=tb, kb=kb: e.dma_start(
                    out=xlnr_d[tb * 128:(tb + 1) * 128, kb * 128:(kb + 1) * 128],
                    in_=otile[:]))
                P.barrier()
        for h in range(2):
            P.on("vector", lambda e, h=h: e.tensor_copy(
                lrow[:, h * 512:h * 512 + 512], pbank[4 + h][0:1, :]))
        P.barrier()
        P.dma("gpsimd", lambda e: e.dma_start(out=lgr_d[:, :], in_=lrow[:]))
        P.barrier()
        P.on("gpsimd", lambda e: e.engine_nop())
        P.build(nc)
    return nc


def _blk(w, KB, MB):
    return np.ascontiguousarray(
        w.reshape(KB, 128, MB, 128).transpose(2, 1, 0, 3))


def prep_l1_weights(inp):
    d = {}
    d["wr"] = np.ascontiguousarray(inp["L1_ffn_w_router"].reshape(16, 128).T)
    d["g1"] = np.ascontiguousarray(inp["L1_ffn_g1"].reshape(16, 128).T)
    d["g2"] = np.ascontiguousarray(inp["L1_ffn_g2"].reshape(16, 128).T)
    d["fwq"] = _blk(inp["L1_ffn_wq"], 16, 16)
    d["fwk"] = _blk(inp["L1_ffn_wk"], 16, 16)
    d["fwv"] = _blk(inp["L1_ffn_wv"], 16, 16)
    d["fwo"] = _blk(inp["L1_ffn_wo"], 16, 16)
    d["fw1"] = _blk(inp["L1_ffn_w1"], 16, 64)
    d["fw3"] = _blk(inp["L1_ffn_w3"], 16, 64)
    d["fw2"] = np.ascontiguousarray(
        inp["L1_ffn_w2"].reshape(4, 16, 128, 16, 128).transpose(3, 0, 2, 1, 4))
    d["maskt"] = np.ascontiguousarray(inp["L1_ffn_mask"].T)
    fc = inp["L1_ffn_freqs_cis"]
    d["ct"] = np.ascontiguousarray(fc[..., 0].T)
    d["st"] = np.ascontiguousarray(fc[..., 1].T)
    d["awq"] = _blk(inp["L1_attn_Wq"], 16, 16)
    d["awk"] = np.ascontiguousarray(
        inp["L1_attn_Wk"].reshape(16, 128, 4, 128).transpose(2, 1, 0, 3))
    d["awv"] = np.ascontiguousarray(
        inp["L1_attn_Wv"].reshape(16, 128, 4, 128).transpose(2, 1, 0, 3))
    d["awo"] = _blk(inp["L1_attn_Wo"], 16, 16)
    d["qg"] = np.ascontiguousarray(inp["L1_attn_q_gamma"][:, None])
    d["kg"] = np.ascontiguousarray(inp["L1_attn_k_gamma"][:, None])
    return d


def run_l1ffn(ncs, xT, w, f1_precomputed=None):
    nc_f1, nc_f15, nc_f2a, nc_f2b, nc_f3 = ncs
    if f1_precomputed is None:
        in_maps = [{"xt": np.ascontiguousarray(xT[:, c * 1024:(c + 1) * 1024]),
                    "wr": w["wr"]} for c in range(NCORE)]
        res = run_bass_kernel_spmd(nc_f1, in_maps, core_ids=list(range(NCORE)))
        xlnr = [np.asarray(r["xlnr"]) for r in res.results]
        lgr = [np.asarray(r["lgr"]) for r in res.results]
        xlnr_b = [np.concatenate(xlnr[4 * b:4 * b + 4], axis=0) for b in range(2)]
        lg_b = [np.concatenate(lgr[4 * b:4 * b + 4], axis=1) for b in range(2)]
    else:
        xlnr_b, lg_b = f1_precomputed
    in_maps = []
    for c in range(NCORE):
        b = c // 4
        in_maps.append({"lg": np.ascontiguousarray(lg_b[b]),
                        "xlnr": np.ascontiguousarray(xlnr_b[b]),
                        "lgc": np.ascontiguousarray(lg_b[b].reshape(4096, 1))})
    res = run_bass_kernel_spmd(nc_f15, in_maps, core_ids=list(range(NCORE)))
    xf_b, sel_b, rw_b = [], [], []
    for b in range(2):
        r = res.results[4 * b]
        xf_b.append(np.asarray(r["xf"]))
        sel_b.append(np.asarray(r["selo"])[:, 0])
        rw_b.append(np.ascontiguousarray(np.asarray(r["rwo"]).T.reshape(1024)))
    xfT_b = [np.ascontiguousarray(xf_b[b].T) for b in range(2)]
    in_maps = []
    for c in range(NCORE):
        b, hq = c // 4, c % 4
        in_maps.append({"xft": xfT_b[b], "maskt": w["maskt"], "ct": w["ct"],
                        "st": w["st"], "g1": w["g1"],
                        "wq": np.ascontiguousarray(w["fwq"][hq * 4:(hq + 1) * 4]),
                        "wk": np.ascontiguousarray(w["fwk"][hq * 4:(hq + 1) * 4]),
                        "wv": np.ascontiguousarray(w["fwv"][hq * 4:(hq + 1) * 4])})
    res = run_bass_kernel_spmd(nc_f2a, in_maps, core_ids=list(range(NCORE)))
    oo = [np.asarray(r["o"]) for r in res.results]
    O_b = [np.concatenate(oo[4 * b:4 * b + 4], axis=0) for b in range(2)]
    in_maps = []
    for c in range(NCORE):
        b, q = c // 4, c % 4
        in_maps.append({
            "xfq": np.ascontiguousarray(xfT_b[b][:, q * 256:(q + 1) * 256]),
            "ofq": np.ascontiguousarray(O_b[b][:, q * 256:(q + 1) * 256]),
            "rwq": np.ascontiguousarray(rw_b[b][q * 256:(q + 1) * 256][None, :]),
            "g2": w["g2"], "wo": w["fwo"], "w1": w["fw1"], "w3": w["fw3"],
            "w2": w["fw2"]})
    res = run_bass_kernel_spmd(nc_f2b, in_maps, core_ids=list(range(NCORE)))
    xw = [np.asarray(r["xwr"]) for r in res.results]
    xw_b = [np.concatenate(xw[4 * b:4 * b + 4], axis=0) for b in range(2)]
    xrows = np.ascontiguousarray(xT.T)
    in_maps = []
    for c in range(NCORE):
        b, q = c // 4, c % 4
        in_maps.append({
            "xr": np.ascontiguousarray(xrows[b * 4096 + q * 1024:
                                             b * 4096 + (q + 1) * 1024]),
            "xlr": np.ascontiguousarray(xlnr_b[b][q * 1024:(q + 1) * 1024]),
            "xwr": xw_b[b],
            "sel": np.ascontiguousarray(sel_b[b][:, None]),
            "r0": np.full((128, 1), q * 1024.0, np.float32)})
    res = run_bass_kernel_spmd(nc_f3, in_maps, core_ids=list(range(NCORE)))
    outrows = np.concatenate([np.asarray(r["out"]) for r in res.results], axis=0)
    return np.ascontiguousarray(outrows.T)


def run_l1attn(ncs, xT, w):
    nc_a1, nc_a2f1 = ncs
    in_maps = []
    for c in range(NCORE):
        b, kvh = c // 4, c % 4
        in_maps.append({
            "xt": np.ascontiguousarray(xT[:, b * 4096:(b + 1) * 4096]),
            "wq": np.ascontiguousarray(w["awq"][kvh * 4:(kvh + 1) * 4]),
            "wk": np.ascontiguousarray(w["awk"][kvh:kvh + 1]),
            "wv": np.ascontiguousarray(w["awv"][kvh:kvh + 1]),
            "qg": w["qg"], "kg": w["kg"]})
    res = run_bass_kernel_spmd(nc_a1, in_maps, core_ids=list(range(NCORE)))
    oo = [np.asarray(r["o"]) for r in res.results]
    O_b = [np.concatenate(oo[4 * b:4 * b + 4], axis=0) for b in range(2)]
    in_maps = []
    for c in range(NCORE):
        b, q = c // 4, c % 4
        in_maps.append({
            "xt": np.ascontiguousarray(xT[:, c * 1024:(c + 1) * 1024]),
            "of": np.ascontiguousarray(O_b[b][:, q * 1024:(q + 1) * 1024]),
            "wo": w["awo"], "wr": w["wr"]})
    res = run_bass_kernel_spmd(nc_a2f1, in_maps, core_ids=list(range(NCORE)))
    newx = np.concatenate([np.asarray(r["out"]) for r in res.results], axis=1)
    xlnr = [np.asarray(r["xlnr"]) for r in res.results]
    lgr = [np.asarray(r["lgr"]) for r in res.results]
    xlnr_b = [np.concatenate(xlnr[4 * b:4 * b + 4], axis=0) for b in range(2)]
    lg_b = [np.concatenate(lgr[4 * b:4 * b + 4], axis=1) for b in range(2)]
    return newx, (xlnr_b, lg_b)


_NC_CACHE = {}


def _get_ncs():
    if "all" not in _NC_CACHE:
        _NC_CACHE["all"] = dict(
            ffn=build_l0ffn(), m1a=build_m1a(), m1b=build_m1b(), m2=build_m2(),
            f1=build_f1(), f15=build_f15(), f2a=build_f2a(), f2b=build_f2b(),
            f3=build_f3(), a1=build_a1(), a2f1=build_a2f1())
    return _NC_CACHE["all"]


def kernel(**inputs):
    inputs = {k: np.asarray(v, dtype=np.float32) if np.asarray(v).dtype == np.float32
              else np.asarray(v) for k, v in inputs.items()}
    x = inputs["L0_attn_x"].astype(np.float32)
    xT = np.ascontiguousarray(x.reshape(8192, 2048).T)
    w0 = prep_l0ffn_weights(inputs)
    wm = prep_mamba_weights(inputs)
    w1 = prep_l1_weights(inputs)
    ncs = _get_ncs()
    xT = run_l0ffn(ncs["ffn"], xT, w0)
    xT = run_mamba((ncs["m1a"], ncs["m1b"], ncs["m2"]), xT, wm)
    xT = run_l0ffn(ncs["ffn"], xT, w0)
    fchain = (ncs["f1"], ncs["f15"], ncs["f2a"], ncs["f2b"], ncs["f3"])
    xT = run_l1ffn(fchain, xT, w1)
    xT, f1pre = run_l1attn((ncs["a1"], ncs["a2f1"]), xT, w1)
    xT = run_l1ffn(fchain, xT, w1, f1_precomputed=f1pre)
    out = np.ascontiguousarray(xT.T).reshape(2, 4096, 2048)
    return out
